# version 11 (final)
# speedup vs baseline: 1.0692x; 1.0427x over previous
import contextlib
import math
import numpy as np
import concourse.bass as bass
import concourse.mybir as mybir
from concourse.alu_op_type import AluOpType
from concourse.bass_utils import run_bass_kernel_spmd

F32 = mybir.dt.float32
AF = mybir.ActivationFunctionType
OP = AluOpType
MAGIC = 12582912.0
T, C = 2048, 1024
LN_EPS = 1e-5

BUILD_PHASE = "full"


class S:
    def __init__(self, handle):
        self.h = handle
        self.n = 0


def build(nc: bass.Bass, phase: str):
    d = {}

    def din(name, shape):
        d[name] = nc.dram_tensor(name, list(shape), F32, kind="ExternalInput").ap()
        return d[name]

    def dint(name, shape):
        d[name] = nc.dram_tensor(name, list(shape), F32, kind="Internal").ap()
        return d[name]

    x_tok = din("x_tok", (T, C))
    din("xT", (C, T))
    for nm in ["wqT", "wkT", "wvT", "woT"]:
        din(nm, (C, C))
    for nm in ["bq", "bk"]:
        din(nm, (128, 8))
    for nm in ["bv", "bo"]:
        din(nm, (1, C))
    din("ident", (128, 128))
    din("onesrow", (33, 128))
    out = nc.dram_tensor("out", [T, C], F32, kind="ExternalOutput").ap()

    attnTs = dint("attnTs", (C, T))
    x1s = dint("x1s", (T, C))
    x2s = dint("x2s", (T, C))
    wupdqT = dint("wupdqT", (C, 3072))
    wgatedqT = dint("wgatedqT", (C, 3072))
    wdowndqT = dint("wdowndqT", (3072, C))
    upTs = dint("upTs", (3072, T))
    gateTs = dint("gateTs", (3072, T))
    g2Ts = dint("g2Ts", (3072, T))
    opreTs = dint("opreTs", (3072, T))
    if phase in ("dbgb", "fullx3"):
        dbg_lng1 = nc.dram_tensor("dbg_lng1", [128, 8 * 2048], F32, kind="ExternalOutput").ap()
        dbg_xb0 = nc.dram_tensor("dbg_xb0", [C, T], F32, kind="ExternalOutput").ap()
        dbg_k = nc.dram_tensor("dbg_k", [C, T], F32, kind="ExternalOutput").ap()
        dbg_v = nc.dram_tensor("dbg_v", [C, T], F32, kind="ExternalOutput").ap()
        dbg_r = nc.dram_tensor("dbg_r", [C, T], F32, kind="ExternalOutput").ap()
        dbg_y0 = nc.dram_tensor("dbg_y0", [128, T], F32, kind="ExternalOutput").ap()
        dbg_bb0 = nc.dram_tensor("dbg_bb0", [128, T], F32, kind="ExternalOutput").ap()
        dbg_xb1 = nc.dram_tensor("dbg_xb1", [128, 8 * 2048], F32, kind="ExternalOutput").ap()
        dbg_kk = nc.dram_tensor("dbg_kk", [4096, T], F32, kind="ExternalOutput").ap()
        dbg_xb2 = nc.dram_tensor("dbg_xb2", [128, 16640], F32, kind="ExternalOutput").ap()
        dbg_fix = nc.dram_tensor("dbg_fix", [128, 8], F32, kind="ExternalOutput").ap()
        dbg_xn = nc.dram_tensor("dbg_xn", [128, 16384], F32, kind="ExternalOutput").ap()
    xb0s = dint("xb0s", (C, T))
    kmixs = dint("kmixs", (C, T))
    vmixs = dint("vmixs", (C, T))
    rmixs = dint("rmixs", (C, T))
    kkms = dint("kkms", (4096, T))
    x3s = dint("x3s", (T, C))
    cc_in = dint("cc_in", (1, 256))
    cc_out = dint("cc_out", (1, 256))
    ccT = dint("ccT", (4, 128))
    for nm in ["w_up", "w_gate"]:
        din(nm, (3072, C))
    din("w_down", (C, 3072))
    din("W_tm", (C, 3072))
    din("W_cm", (C, 4096))
    din("W_cp", (4096, C))
    din("prm", (128, 96))
    din("iota", (128, T))
    if phase == "dbg":
        dbg_ln0 = nc.dram_tensor("dbg_ln0", [128, 1024], F32, kind="ExternalOutput").ap()
        dbg_st0 = nc.dram_tensor("dbg_st0", [128, 40], F32, kind="ExternalOutput").ap()
        dbg_lnt = nc.dram_tensor("dbg_lnt", [128, 8 * 2048], F32, kind="ExternalOutput").ap()
        dbg_vaug = nc.dram_tensor("dbg_vaug", [128, 16 * 1040], F32, kind="ExternalOutput").ap()
        dbg_q = nc.dram_tensor("dbg_q", [128, 2048], F32, kind="ExternalOutput").ap()
        dbg_k = nc.dram_tensor("dbg_k", [128, 2048], F32, kind="ExternalOutput").ap()
        dbg_at = nc.dram_tensor("dbg_at", [C, T], F32, kind="ExternalOutput").ap()

    x_tok_t = x_tok.rearrange("(n p) c -> n p c", p=128)
    out_t = out.rearrange("(n p) c -> n p c", p=128)
    attnTs_t = attnTs.rearrange("(n p) t -> n p t", p=128)
    x1s_t = x1s.rearrange("(n p) c -> n p c", p=128)

    es = contextlib.ExitStack()
    with es:
        IDENT = es.enter_context(nc.sbuf_tensor("IDENT", [128, 128], F32))
        BCOLS = es.enter_context(nc.sbuf_tensor("BCOLS", [128, 32], F32))
        ONESROW = es.enter_context(nc.sbuf_tensor("ONESROW", [33, 128], F32))
        BROW2 = es.enter_context(nc.sbuf_tensor("BROW2", [33, 1024], F32))
        STAT = es.enter_context(nc.sbuf_tensor("STAT", [128, 32], F32))
        MCOL = es.enter_context(nc.sbuf_tensor("MCOL", [128, 8], F32))
        JUNK = es.enter_context(nc.sbuf_tensor("JUNK", [128, 640], F32))
        FMAX = es.enter_context(nc.sbuf_tensor("FMAX", [128, 64], F32))
        PRM = es.enter_context(nc.sbuf_tensor("PRM", [128, 96], F32))
        SCL = es.enter_context(nc.sbuf_tensor("SCL", [128, 32], F32))
        CCS = es.enter_context(nc.sbuf_tensor("CCS", [1, 256], F32))
        CCR = es.enter_context(nc.sbuf_tensor("CCR", [1, 256], F32))
        CCT = es.enter_context(nc.sbuf_tensor("CCT", [1, 16], F32))
        WR0 = es.enter_context(nc.sbuf_tensor("WR0", [128, 2048], F32))
        WR1 = es.enter_context(nc.sbuf_tensor("WR1", [128, 1024], F32))
        LNT = es.enter_context(nc.sbuf_tensor("LNT", [128, 8 * 2048], F32))
        VAUG = es.enter_context(nc.sbuf_tensor("VAUG", [128, 16 * 1040], F32))
        EXP = es.enter_context(nc.sbuf_tensor("EXP", [128, 16 * 512], F32))
        PQ = es.enter_context(nc.sbuf_tensor("PQ", [128, 2048], F32))
        PK = es.enter_context(nc.sbuf_tensor("PK", [128, 2048], F32))
        PA = es.enter_context(nc.sbuf_tensor("PA", [128, 2048], F32))
        PSD = es.enter_context(nc.psum_tensor("PSD", [128, 1024], F32))
        PS2 = es.enter_context(nc.psum_tensor("PS2", [128, 512], F32))
        PS3 = es.enter_context(nc.psum_tensor("PS3", [128, 512], F32))
        PPV0 = es.enter_context(nc.psum_tensor("PPV0", [128, 260], F32))
        PPV1 = es.enter_context(nc.psum_tensor("PPV1", [128, 260], F32))
        PPVS = [PPV0, PPV1]
        h_ld = es.enter_context(nc.semaphore("s_ld"))
        h_mm = es.enter_context(nc.semaphore("s_mm"))
        h_sc = es.enter_context(nc.semaphore("s_sc"))
        h_ve = es.enter_context(nc.semaphore("s_ve"))
        h_st = es.enter_context(nc.semaphore("s_st"))
        h_gp = es.enter_context(nc.semaphore("s_gp"))
        h_gs = es.enter_context(nc.semaphore("s_gs"))
        block = es.enter_context(nc.Block())
        ld, mm, sc, ve, st, gp = S(h_ld), S(h_mm), S(h_sc), S(h_ve), S(h_st), S(h_gp)
        gs = S(h_gs)
        PS0 = PSD[:, 0:512]
        PS1 = PSD[:, 512:1024]
        PSS = [PS0, PS1]
        XROFF = [0, 1024]
        LROFF = [0, 1024]
        G = {}

        def vgap(e):
            e.tensor_copy(out=JUNK[:, 0:128], in_=JUNK[:, 0:128])

        def sgap(e):
            e.copy(out=JUNK[:, 0:128], in_=JUNK[:, 0:128])

        def pre_loads(e):
            e.dma_start(out=IDENT[:], in_=d["ident"]).then_inc(ld.h, 16)
            e.dma_start(out=ONESROW[:], in_=d["onesrow"]).then_inc(ld.h, 16)
            for i, nm in enumerate(["bq", "bk"]):
                e.dma_start(out=BCOLS[:, 8 * i:8 * (i + 1)], in_=d[nm]).then_inc(ld.h, 16)
            e.dma_start(out=BROW2[0:1, :], in_=d["bv"]).then_inc(ld.h, 16)
            e.dma_start(out=PRM[:, 0:88], in_=d["prm"][:, 0:88]).then_inc(ld.h, 16)
            e.dma_start(out=BROW2[32:33, :], in_=d["bo"]).then_inc(ld.h, 16)
        block.sync(pre_loads)
        ld.n += 16 * 7
        LD_PRE = ld.n

        MM_PRE = mm.n
        G["pre_evac"] = sc.n

        def pre_junk(e):
            e.memset(JUNK[:], 0.0)
            e.memset(CCS[:], 0.0)
            e.memset(CCT[:], 0.0)
        block.vector(pre_junk)

        for tt in range(16):
            xo, lo = XROFF[tt % 2], LROFF[tt % 2]

            def a1_load(e, tt=tt, xo=xo):
                if tt == 0:
                    e.wait_ge(ve.h, ve.n)
                    e.wait_ge(sc.h, sc.n)
                if tt >= 2:
                    e.wait_ge(ve.h, G[("a1_apply", tt - 2)])
                e.dma_start(out=PK[:, xo:xo + 1024], in_=x_tok_t[tt]).then_inc(ld.h, 16)
            block.sync(a1_load)
            ld.n += 16
            ld_x = ld.n

            def a1_sum(e, xo=xo, ld_x=ld_x, tt=tt):
                e.wait_ge(ld.h, ld_x + (16 if tt < 15 else 0))
                e.tensor_scalar(out=EXP[:, 1024:2048], in0=PK[:, xo:xo + 1024], scalar1=1.0,
                                scalar2=None, op0=OP.mult, op1=OP.add,
                                accum_out=STAT[:, 26:27])
                vgap(e)
                e.tensor_copy(out=JUNK[:, 0:128], in_=JUNK[:, 0:128]).then_inc(ve.h, 1)
            block.vector(a1_sum)
            ve.n += 1
            ve_sum = ve.n

            def a1_sq(e, xo=xo, ld_x=ld_x, tt=tt):
                e.wait_ge(ld.h, ld_x + (16 if tt < 15 else 0))
                e.activation(out=EXP[:, 0:1024], in_=PK[:, xo:xo + 1024], func=AF.Square,
                             accum_out=STAT[:, 27:28])
                sgap(e)
                e.copy(out=JUNK[:, 0:128], in_=JUNK[:, 0:128]).then_inc(sc.h, 1)
            block.scalar(a1_sq)
            sc.n += 1
            sc_sq = sc.n

            def a1_comb(e, sc_sq=sc_sq):
                e.wait_ge(sc.h, sc_sq)
                e.tensor_scalar(out=STAT[:, 24:25], in0=STAT[:, 26:27], scalar1=1.0 / 1024,
                                scalar2=None, op0=OP.mult)
                e.tensor_scalar(out=MCOL[:, 2:3], in0=STAT[:, 27:28], scalar1=1.0 / 1024,
                                scalar2=None, op0=OP.mult)
                vgap(e)
                e.tensor_tensor(out=MCOL[:, 0:1], in0=STAT[:, 24:25], in1=STAT[:, 24:25],
                                op=OP.mult)
                vgap(e)
                e.tensor_tensor(out=MCOL[:, 1:2], in0=MCOL[:, 2:3], in1=MCOL[:, 0:1],
                                op=OP.subtract)
                vgap(e)
                e.tensor_scalar(out=MCOL[:, 1:2], in0=MCOL[:, 1:2], scalar1=LN_EPS,
                                scalar2=None, op0=OP.add)
                vgap(e)
                e.tensor_copy(out=JUNK[:, 0:128], in_=JUNK[:, 0:128]).then_inc(ve.h, 1)
            block.vector(a1_comb)
            ve.n += 1
            ve_stats = ve.n

            def a1_rstd(e, ve_stats=ve_stats):
                e.wait_ge(ve.h, ve_stats)
                e.activation(out=MCOL[:, 3:4], in_=MCOL[:, 1:2], func=AF.Ln)
                sgap(e)
                e.activation(out=MCOL[:, 5:6], in_=MCOL[:, 3:4], func=AF.Exp, scale=-0.5)
                sgap(e)
                e.copy(out=JUNK[:, 0:128], in_=JUNK[:, 0:128]).then_inc(sc.h, 1)
            block.scalar(a1_rstd)
            sc.n += 1
            sc_sqrt = sc.n

            def a1_apply(e, tt=tt, xo=xo, lo=lo, sc_sqrt=sc_sqrt):
                e.wait_ge(sc.h, sc_sqrt)
                if tt >= 2:
                    e.wait_ge(mm.h, G[("a1_tr", tt - 2)])
                e.tensor_scalar(out=PA[:, lo:lo + 1024], in0=PK[:, xo:xo + 1024],
                                scalar1=STAT[:, 24:25],
                                scalar2=MCOL[:, 5:6], op0=OP.subtract,
                                op1=OP.mult).then_inc(ve.h, 1)
            block.vector(a1_apply)
            ve.n += 1
            G[("a1_apply", tt)] = ve.n
            ve_apply = ve.n
            if phase == "dbg" and tt == 0:
                def dbg_a1(e, ve_apply=ve_apply):
                    e.wait_ge(ve.h, ve_apply)
                    e.dma_start(out=dbg_ln0[:], in_=PA[:, 0:1024]).then_inc(st.h, 16)
                    e.dma_start(out=dbg_st0[:, 0:32], in_=STAT[:]).then_inc(st.h, 16)
                    e.dma_start(out=dbg_st0[:, 32:40], in_=MCOL[:]).then_inc(st.h, 16)
                block.sync(dbg_a1)
                st.n += 48

            def a1_tr(e, tt=tt, lo=lo, ve_apply=ve_apply):
                e.wait_ge(ve.h, ve_apply)
                if tt >= 1:
                    e.wait_ge(sc.h, G[("a1_trev", tt - 1)])
                for half in range(2):
                    ps = [PS2, PS3][half]
                    for j in range(4):
                        ct = 4 * half + j
                        ins = e.transpose(ps[:, 128 * j:128 * (j + 1)],
                                          PA[:, lo + 128 * ct:lo + 128 * (ct + 1)], IDENT[:])
                ins.then_inc(mm.h, 1)
            block.tensor(a1_tr)
            mm.n += 1
            G[("a1_tr", tt)] = mm.n
            mm_tr = mm.n

            def a1_trev(e, tt=tt, mm_tr=mm_tr):
                e.wait_ge(mm.h, mm_tr)
                for half in range(2):
                    ps = [PS2, PS3][half]
                    dst = LNT[:].rearrange("p (ct t) -> p ct t", t=2048)[
                        :, 4 * half:4 * half + 4, 128 * tt:128 * (tt + 1)]
                    ins = e.copy(out=dst, in_=ps[:].rearrange("p (b t) -> p b t", t=128))
                ins.then_inc(sc.h, 1)
            block.scalar(a1_trev)
            sc.n += 1
            G[("a1_trev", tt)] = sc.n
        SC_LNT = sc.n

        def a2_wv(e):
            e.wait_ge(sc.h, sc.n)
            e.wait_ge(ve.h, ve.n)
            wvt = d["wvT"].rearrange("(n p) e -> n p e", p=128)
            for ck in range(8):
                e.dma_start(out=EXP[:, 1024 * ck:1024 * (ck + 1)], in_=wvt[ck]).then_inc(ld.h, 16)
        block.sync(a2_wv)
        ld.n += 16 * 8
        LD_WV = ld.n

        def a2_ones(e):
            vo = VAUG[:].rearrange("p (q h u) -> p q h u", h=16, u=65)[:, :, :, 64:65]
            e.memset(vo, 1.0).then_inc(ve.h, 1)
        block.vector(a2_ones)
        ve.n += 1

        for tt in range(16):
            for eh in range(2):
                it = 2 * tt + eh
                ps = PSS[it % 2]

                def a2_mm(e, tt=tt, eh=eh, ps=ps, it=it):
                    if it == 0:
                        e.wait_ge(ld.h, LD_WV + 32)
                        e.wait_ge(sc.h, SC_LNT)
                    if it >= 2:
                        e.wait_ge(ve.h, G[("a2_ev", it - 2)])
                    for ct in range(8):
                        e.matmul(
                            ps[:], LNT[:, 2048 * ct + 128 * tt:2048 * ct + 128 * (tt + 1)],
                            EXP[:, 1024 * ct + 512 * eh:1024 * ct + 512 * (eh + 1)],
                            start=(ct == 0), stop=False)
                    ins = e.matmul(ps[:], ONESROW[0:1, :], BROW2[0:1, 512 * eh:512 * (eh + 1)],
                                   start=False, stop=True)
                    ins.then_inc(mm.h, 1)
                block.tensor(a2_mm)
                mm.n += 1
                mm_a2 = mm.n

                def a2_ev(e, tt=tt, eh=eh, ps=ps, mm_a2=mm_a2):
                    e.wait_ge(mm.h, mm_a2)
                    dst = VAUG[:].rearrange("p (q h u) -> p q h u", h=16, u=65)[
                        :, tt, 8 * eh:8 * (eh + 1), 0:64]
                    e.tensor_copy(out=dst,
                                  in_=ps[:].rearrange("p (h u) -> p h u", u=64)).then_inc(ve.h, 1)
                block.vector(a2_ev)
                ve.n += 1
                G[("a2_ev", it)] = ve.n
        VE_VAUG = ve.n

        if phase == "dbg":
            def dbg1(e):
                e.wait_ge(sc.h, SC_LNT)
                e.dma_start(out=dbg_lnt[:], in_=LNT[:]).then_inc(st.h, 16)
                e.wait_ge(ve.h, VE_VAUG)
                e.dma_start(out=dbg_vaug[:], in_=VAUG[:]).then_inc(st.h, 16)
            block.sync(dbg1)
            st.n += 32

        exp_sc = {}
        g_sc = 0

        def emit_wld(hp):
            def a3_wld(e, hp=hp):
                if hp >= 1:
                    e.wait_ge(mm.h, G[("a3_qk_done", hp - 1)])
                wq4 = d["wqT"].rearrange("(ct p) (h j) -> p ct h j", p=128, j=128)
                wk4 = d["wkT"].rearrange("(ct p) (h j) -> p ct h j", p=128, j=128)
                e.dma_start(out=WR0[:, 0:1024].rearrange("p (ct j) -> p ct j", j=128),
                            in_=wq4[:, :, hp, :]).then_inc(ld.h, 16)
                e.dma_start(out=WR1[:, 0:1024].rearrange("p (ct j) -> p ct j", j=128),
                            in_=wk4[:, :, hp, :]).then_inc(ld.h, 16)
            block.sync(a3_wld)
            ld.n += 32
            G[("a3_ldqk", hp)] = ld.n

        def emit_qkproj(hp):
            ld_qk = G[("a3_ldqk", hp)]
            for which, wbuf, dst, bcol_off in ((0, WR0, PQ, 0), (1, WR1, PK, 8)):
                for tc in range(4):
                    it = 4 * which + tc
                    ps = PSS[it % 2]

                    def a3_qk(e, wbuf=wbuf, tc=tc, ps=ps, it=it, hp=hp, ld_qk=ld_qk):
                        e.wait_ge(ld.h, ld_qk)
                        if it == 0 and hp == 0:
                            e.wait_ge(ve.h, VE_VAUG)
                        if it < 2 and hp > 0:
                            e.wait_ge(sc.h, G["exp_last"])
                        if it >= 2:
                            e.wait_ge(sc.h, G[("a3_qkev", hp, it - 2)])
                        for ct in range(8):
                            ins = e.matmul(
                                ps[:], wbuf[:, 128 * ct:128 * (ct + 1)],
                                LNT[:, 2048 * ct + 512 * tc:2048 * ct + 512 * (tc + 1)],
                                start=(ct == 0), stop=(ct == 7))
                        ins.then_inc(mm.h, 1)
                    block.tensor(a3_qk)
                    mm.n += 1
                    mm_qk = mm.n
                    if it == 7:
                        G[("a3_qk_done", hp)] = mm.n

                    def a3_qkev(e, dst=dst, tc=tc, ps=ps, hp=hp, bcol_off=bcol_off, mm_qk=mm_qk):
                        e.wait_ge(mm.h, mm_qk)
                        e.activation(out=dst[:, 512 * tc:512 * (tc + 1)], in_=ps[:],
                                     func=AF.Identity, bias=BCOLS[:, bcol_off + hp:bcol_off + hp + 1],
                                     scale=1.0).then_inc(sc.h, 1)
                    block.scalar(a3_qkev)
                    sc.n += 1
                    G[("a3_qkev", hp, it)] = sc.n
            G[("a3_scqk", hp)] = sc.n

        emit_wld(0)
        emit_qkproj(0)

        for hp in range(8):
            if hp + 1 < 8:
                emit_wld(hp + 1)
            SC_QK = G[("a3_scqk", hp)]
            if phase == "dbg" and hp == 0:
                def dbgqk(e, SC_QK=SC_QK):
                    e.wait_ge(sc.h, SC_QK)
                    e.dma_start(out=dbg_q[:], in_=PQ[:]).then_inc(st.h, 16)
                    e.dma_start(out=dbg_k[:], in_=PK[:]).then_inc(st.h, 16)
                block.sync(dbgqk)
                st.n += 32

            for hh in range(2):
                rowlo = 64 * hh
                for t1c in range(4):
                    X = 4 * hh + t1c
                    par = X % 2
                    PPVX = PPVS[par]

                    def emit_pvg(j, hp=hp, hh=hh, X=X, par=par, PPVX=PPVX):
                        def a3_pvg(e, j=j, hp=hp, hh=hh, X=X, par=par, PPVX=PPVX):
                            if j == 0:
                                e.wait_ge(ve.h, G.get(("pv_div", par), 0))
                            if j >= 14:
                                e.wait_ge(sc.h, exp_sc[(hp, hh, X - 4 * hh, j)])
                            for s4 in range(4):
                                ins = e.matmul(
                                    PPVX[:, 65 * s4:65 * (s4 + 1)],
                                    EXP[:, 512 * j + 128 * s4:512 * j + 128 * (s4 + 1)],
                                    VAUG[:, 1040 * j + 65 * (2 * hp + hh):
                                         1040 * j + 65 * (2 * hp + hh) + 65],
                                    start=(j == 0 and s4 == 0),
                                    stop=(j == 15 and s4 == 3))
                            ins.then_inc(mm.h, 1)
                        block.tensor(a3_pvg)
                        mm.n += 1

                    for t2t in range(16):
                        ps = PSS[g_sc % 2]

                        def a3_sc(e, hp=hp, rowlo=rowlo, t1c=t1c, t2t=t2t, ps=ps,
                                  g=g_sc, SC_QK=SC_QK):
                            e.wait_ge(sc.h, SC_QK)
                            if g >= 2:
                                e.wait_ge(sc.h, G[("exp_g", g - 2)])
                            ins = e.matmul(ps[:], PK[rowlo:rowlo + 64, 128 * t2t:128 * (t2t + 1)],
                                           PQ[rowlo:rowlo + 64, 512 * t1c:512 * (t1c + 1)],
                                           start=True, stop=True)
                            ins.then_inc(mm.h, 1)
                        block.tensor(a3_sc)
                        mm.n += 1
                        mm_sc = mm.n

                        def a3_exp(e, t2t=t2t, ps=ps, mm_sc=mm_sc):
                            e.wait_ge(mm.h, mm_sc)
                            e.activation(out=EXP[:, 512 * t2t:512 * (t2t + 1)], in_=ps[:],
                                         func=AF.Exp, scale=0.125).then_inc(sc.h, 1)
                        block.scalar(a3_exp)
                        sc.n += 1
                        G[("exp_g", g_sc)] = sc.n
                        exp_sc[(hp, hh, t1c, t2t)] = sc.n
                        G["exp_last"] = sc.n
                        g_sc += 1

                        if t2t >= 2:
                            emit_pvg(t2t - 2)
                    emit_pvg(14)
                    emit_pvg(15)
                    mm_pv = mm.n

                    def a3_rden(e, mm_pv=mm_pv, par=par, PPVX=PPVX):
                        e.wait_ge(mm.h, mm_pv)
                        den = PPVX[:].rearrange("p (s u) -> p s u", u=65)[:, :, 64]
                        e.activation(out=STAT[:, 8 * par:8 * par + 4], in_=den, func=AF.Ln)
                        sgap(e)
                        e.activation(out=STAT[:, 8 * par + 4:8 * par + 8],
                                     in_=STAT[:, 8 * par:8 * par + 4], func=AF.Exp,
                                     scale=-1.0)
                        sgap(e)
                        e.copy(out=JUNK[:, 0:128], in_=JUNK[:, 0:128]).then_inc(sc.h, 1)
                    block.scalar(a3_rden)
                    sc.n += 1
                    sc_rden = sc.n

                    def a3_div(e, hh=hh, t1c=t1c, sc_rden=sc_rden, par=par, PPVX=PPVX, hp=hp):
                        e.wait_ge(sc.h, sc_rden)
                        if hh == 0 and t1c == 0 and hp > 0:
                            e.wait_ge(gs.h, G[("a3_spill", hp - 1)])
                        for s4 in range(4):
                            e.tensor_scalar(
                                out=PA[:, 128 * (4 * t1c + s4) + 64 * hh:
                                       128 * (4 * t1c + s4) + 64 * hh + 64],
                                in0=PPVX[:, 65 * s4:65 * s4 + 64],
                                scalar1=STAT[:, 8 * par + 4 + s4:8 * par + 5 + s4], scalar2=None,
                                op0=OP.mult)
                        vgap(e)
                        e.tensor_copy(out=JUNK[:, 0:128],
                                      in_=JUNK[:, 0:128]).then_inc(ve.h, 1)
                    block.vector(a3_div)
                    ve.n += 1
                    G[("pv_div", par)] = ve.n
            VE_PA = ve.n

            if hp + 1 < 8:
                emit_qkproj(hp + 1)

            base_tr = mm.n

            def a3_patr(e, hp=hp, VE_PA=VE_PA, base_tr=base_tr):
                e.wait_ge(ve.h, VE_PA)
                if hp >= 1:
                    e.wait_ge(sc.h, G[("a3_paev_done", hp - 1)])
                for quad in range(4):
                    ps = [PS2, PS3][quad % 2]
                    if quad >= 2:
                        e.wait_ge(sc.h, G[("a3_paev", hp, quad - 2)])
                    for j in range(4):
                        blk = 4 * quad + j
                        ins = e.transpose(ps[:, 128 * j:128 * (j + 1)],
                                          PA[:, 128 * blk:128 * (blk + 1)], IDENT[:])
                    ins.then_inc(mm.h, 1)
            for quad in range(4):
                G[("a3_paev", hp, quad)] = sc.n + quad + 1
            block.tensor(a3_patr)
            mm.n += 4

            for quad in range(4):
                ps = [PS2, PS3][quad % 2]

                def a3_paev(e, hp=hp, quad=quad, ps=ps, base_tr=base_tr):
                    e.wait_ge(mm.h, base_tr + quad + 1)
                    e.copy(out=PA[:, 512 * quad:512 * (quad + 1)], in_=ps[:]).then_inc(sc.h, 1)
                block.scalar(a3_paev)
                sc.n += 1
                G[("a3_paev", hp, quad)] = sc.n
            G[("a3_paev_done", hp)] = sc.n
            sc_pa = sc.n

            def a3_spill(e, hp=hp, sc_pa=sc_pa):
                e.wait_ge(sc.h, sc_pa)
                e.dma_start(out=attnTs_t[hp], in_=PA[:]).then_inc(gs.h, 16)
            block.gpsimd(a3_spill)
            gs.n += 16
            G[("a3_spill", hp)] = gs.n
        ST_ATTNT = gs.n
        if phase == "dbg":
            def dbgat(e):
                e.wait_ge(st.h, ST_ATTNT)
                for ct in range(8):
                    e.dma_start(out=dbg_at.rearrange("(n p) t -> n p t", p=128)[ct],
                                in_=attnTs_t[ct]).then_inc(st.h, 16)
            block.sync(dbgat)
            st.n += 16 * 8

        def a4_wo(e):
            e.wait_ge(mm.h, mm.n)
            wot = d["woT"].rearrange("(n p) e -> n p e", p=128)
            for ck in range(8):
                e.dma_start(out=EXP[:, 1024 * ck:1024 * (ck + 1)], in_=wot[ck]).then_inc(ld.h, 16)
        block.sync(a4_wo)
        ld.n += 16 * 8
        LD_WO = ld.n

        for tt in range(16):
            xo = XROFF[tt % 2]
            wrh = 1024 * (tt % 2)

            def a4_ld(e, tt=tt, xo=xo, wrh=wrh):
                e.wait_ge(gs.h, ST_ATTNT)
                if tt >= 2:
                    e.wait_ge(ve.h, G[("a4_ev", 2 * (tt - 2) + 1)])
                    e.wait_ge(gs.h, G[("a4_st", tt - 2)])
                at4 = attnTs.rearrange("(dc p) (q j) -> p dc q j", p=128, j=128)
                e.dma_start(out=WR0[:, wrh:wrh + 1024].rearrange("p (dc j) -> p dc j", j=128),
                            in_=at4[:, :, tt, :]).then_inc(ld.h, 16)
                e.dma_start(out=PK[:, xo:xo + 1024], in_=x_tok_t[tt]).then_inc(ld.h, 16)
            block.sync(a4_ld)
            ld.n += 32
            ld_a4 = ld.n

            for eh in range(2):
                it = 2 * tt + eh
                ps = PSS[eh]

                def a4_mm(e, tt=tt, eh=eh, ps=ps, it=it, ld_a4=ld_a4, wrh=wrh):
                    if it == 0:
                        e.wait_ge(ld.h, LD_WO)
                    e.wait_ge(ld.h, ld_a4 + (32 if tt < 15 else 0))
                    if it >= 2:
                        e.wait_ge(ve.h, G[("a4_ev", it - 2)])
                    for dc in range(8):
                        e.matmul(ps[:], WR0[:, wrh + 128 * dc:wrh + 128 * (dc + 1)],
                                 EXP[:, 1024 * dc + 512 * eh:1024 * dc + 512 * (eh + 1)],
                                 start=(dc == 0), stop=False)
                    ins = e.matmul(ps[:], ONESROW[32:33, :], BROW2[32:33, 512 * eh:512 * (eh + 1)],
                                   start=False, stop=True)
                    ins.then_inc(mm.h, 1)
                block.tensor(a4_mm)
                mm.n += 1
                mm_a4 = mm.n

                def a4_ev(e, eh=eh, ps=ps, xo=xo, mm_a4=mm_a4):
                    e.wait_ge(mm.h, mm_a4)
                    e.tensor_tensor(out=PK[:, xo + 512 * eh:xo + 512 * (eh + 1)],
                                    in0=ps[:],
                                    in1=PK[:, xo + 512 * eh:xo + 512 * (eh + 1)],
                                    op=OP.add).then_inc(ve.h, 1)
                block.vector(a4_ev)
                ve.n += 1
                G[("a4_ev", it)] = ve.n
            ve_x1 = ve.n

            def a4_st(e, tt=tt, xo=xo, ve_x1=ve_x1):
                e.wait_ge(ve.h, ve_x1)
                e.dma_start(out=x1s_t[tt], in_=PK[:, xo:xo + 1024]).then_inc(gs.h, 16)
                if phase in ("x1", "dbg"):
                    e.dma_start(out=out_t[tt], in_=PK[:, xo:xo + 1024]).then_inc(gs.h, 16)
            block.gpsimd(a4_st)
            gs.n += 32 if phase == "x1" else 16
            G[("a4_st", tt)] = gs.n


        MM_A_END, SC_A_END, VE_A_END, LD_A_END, ST_A_END = mm.n, sc.n, ve.n, ld.n, st.n
        JUNK2 = EXP

        upTs_t = upTs.rearrange("(n p) t -> n p t", p=128)
        gateTs_t = gateTs.rearrange("(n p) t -> n p t", p=128)
        g2Ts_t = g2Ts.rearrange("(n p) t -> n p t", p=128)
        opreTs_t = opreTs.rearrange("(n p) t -> n p t", p=128)
        x2s_t = x2s.rearrange("(n p) c -> n p c", p=128)
        x3s_t = x3s.rearrange("(n p) c -> n p c", p=128)

        def emit_wdq():
            specs = [("w_up", 3072, 1024, wupdqT), ("w_gate", 3072, 1024, wgatedqT),
                     ("w_down", 1024, 3072, wdowndqT)]
            first = True
            prev_nm = None
            for nm, O, I, outT in specs:
                nb = I // 32
                src = d[nm].rearrange("(n p) i -> n p i", p=128)
                outT_r = outT.rearrange("(n p) o -> n p o", p=128)
                npt = O // 128

                def wdq_ld0(e, nm=nm, src=src, first=first, I=I, prev_nm=prev_nm):
                    if first:
                        e.wait_ge(mm.h, MM_A_END)
                        e.wait_ge(ve.h, VE_A_END)
                        e.wait_ge(sc.h, SC_A_END)
                        e.wait_ge(st.h, ST_A_END)
                    else:
                        e.wait_ge(mm.h, G[("wdq_mm_all", prev_nm)])
                    e.dma_start(out=VAUG[:, 0:I], in_=src[0]).then_inc(ld.h, 16)
                block.sync(wdq_ld0)
                ld.n += 16
                first = False
                G[("wdq_ld", nm, 0)] = ld.n

                for pt in range(npt):
                    bo = 3072 * (pt % 2)
                    wo = 8192 + 3072 * (pt % 2)
                    if pt + 1 < npt:
                        def wdq_lda(e, nm=nm, pt=pt, src=src, I=I):
                            if G.get(("wdq_mm", nm, pt - 1)) is not None:
                                e.wait_ge(mm.h, G[("wdq_mm", nm, pt - 1)])
                            e.dma_start(out=VAUG[:, 3072 * ((pt + 1) % 2):
                                            3072 * ((pt + 1) % 2) + I],
                                        in_=src[pt + 1]).then_inc(ld.h, 16)
                        block.sync(wdq_lda)
                        ld.n += 16
                        G[("wdq_ld", nm, pt + 1)] = ld.n
                    ld_w = G[("wdq_ld", nm, min(pt + 1, npt - 1))]

                    def wdq_s(e, I=I, nb=nb, bo=bo, ld_w=ld_w):
                        e.wait_ge(ld.h, ld_w)
                        x3 = VAUG[:, bo:bo + I].rearrange("p (b u) -> p b u", u=32)
                        e.tensor_reduce(out=JUNK2[:, 6528:6528 + nb].unsqueeze(2), in_=x3,
                                        axis=mybir.AxisListType.X, op=OP.max,
                                        apply_absolute_value=True)
                        vgap(e)
                        e.tensor_scalar(out=JUNK2[:, 6144:6144 + nb], in0=JUNK2[:, 6528:6528 + nb],
                                        scalar1=1.0 / 7, scalar2=1e-12, op0=OP.mult,
                                        op1=OP.max)
                        vgap(e)
                        e.tensor_copy(out=JUNK[:, 0:128], in_=JUNK[:, 0:128]).then_inc(ve.h, 1)
                    block.vector(wdq_s)
                    ve.n += 1
                    ve_s = ve.n

                    def wdq_r(e, nb=nb, ve_s=ve_s):
                        e.wait_ge(ve.h, ve_s)
                        e.activation(out=JUNK2[:, 6400:6400 + nb],
                                     in_=JUNK2[:, 6144:6144 + nb], func=AF.Ln)
                        sgap(e)
                        e.activation(out=JUNK2[:, 6272:6272 + nb],
                                     in_=JUNK2[:, 6400:6400 + nb], func=AF.Exp, scale=-1.0)
                        sgap(e)
                        e.copy(out=JUNK[:, 0:128], in_=JUNK[:, 0:128]).then_inc(sc.h, 1)
                    block.scalar(wdq_r)
                    sc.n += 1
                    sc_r = sc.n

                    def wdq_q(e, I=I, nb=nb, bo=bo, wo=wo, sc_r=sc_r, nm=nm, pt=pt):
                        e.wait_ge(sc.h, sc_r)
                        if G.get(("wdq_mm", nm, pt - 2)) is not None:
                            e.wait_ge(mm.h, G[("wdq_mm", nm, pt - 2)])
                        w3 = VAUG[:, bo:bo + I].rearrange("p (b u) -> p b u", u=32)
                        d3 = VAUG[:, wo:wo + I].rearrange("p (b u) -> p b u", u=32)
                        rb = JUNK2[:, 6272:6272 + nb].unsqueeze(2).broadcast_to((128, nb, 32))
                        sb = JUNK2[:, 6144:6144 + nb].unsqueeze(2).broadcast_to((128, nb, 32))
                        e.tensor_tensor(out=d3, in0=w3, in1=rb, op=OP.mult)
                        e.tensor_scalar(out=VAUG[:, bo:bo + I], in0=VAUG[:, wo:wo + I],
                                        scalar1=MAGIC, scalar2=MAGIC, op0=OP.add,
                                        op1=OP.subtract)
                        e.tensor_scalar(out=VAUG[:, wo:wo + I], in0=VAUG[:, bo:bo + I],
                                        scalar1=-8.0, scalar2=7.0, op0=OP.max, op1=OP.min)
                        e.tensor_tensor(out=d3, in0=d3, in1=sb, op=OP.mult).then_inc(ve.h, 1)
                    block.vector(wdq_q)
                    ve.n += 1
                    ve_q = ve.n

                    nblk = I // 128
                    for half in range(nblk // 4):
                        ps = [PS2, PS3][half % 2]

                        def wdq_tr(e, half=half, ps=ps, wo=wo, ve_q=ve_q, nm=nm, pt=pt):
                            e.wait_ge(ve.h, ve_q)
                            gk = ("wdq_ev", nm, pt, half - 2)
                            if G.get(gk) is not None:
                                e.wait_ge(sc.h, G[gk])
                            elif half < 2:
                                e.wait_ge(sc.h, G.get(("wdq_evp",), SC_A_END))
                            for j in range(4):
                                ib = 4 * half + j
                                ins = e.transpose(ps[:, 128 * j:128 * (j + 1)],
                                                  VAUG[:, wo + 128 * ib:wo + 128 * (ib + 1)],
                                                  IDENT[:])
                            ins.then_inc(mm.h, 1)
                        block.tensor(wdq_tr)
                        mm.n += 1
                        mm_tr2 = mm.n

                        def wdq_ev(e, half=half, ps=ps, mm_tr2=mm_tr2, nm=nm, pt=pt):
                            e.wait_ge(mm.h, mm_tr2)
                            if G.get(("wdq_st", nm, pt, half - 2)) is not None:
                                e.wait_ge(st.h, G[("wdq_st", nm, pt, half - 2)])
                            e.copy(out=PA[:, 512 * (half % 2):512 * (half % 2 + 1)],
                                   in_=ps[:]).then_inc(sc.h, 1)
                        block.scalar(wdq_ev)
                        sc.n += 1
                        G[("wdq_ev", nm, pt, half)] = sc.n
                        sc_ev = sc.n

                        def wdq_st(e, half=half, pt=pt, outT_r=outT_r, sc_ev=sc_ev):
                            e.wait_ge(sc.h, sc_ev)
                            for j in range(4):
                                ib = 4 * half + j
                                e.dma_start(
                                    out=outT_r[ib][:, 128 * pt:128 * (pt + 1)],
                                    in_=PA[:, 512 * (half % 2) + 128 * j:
                                           512 * (half % 2) + 128 * (j + 1)]).then_inc(st.h, 16)
                        block.sync(wdq_st)
                        st.n += 64
                        G[("wdq_st", nm, pt, half)] = st.n
                    G[("wdq_evp",)] = sc.n
                    G[("wdq_mm", nm, pt)] = mm.n
                G[("wdq_mm_all", nm)] = mm.n
                prev_nm = nm
            G["wdq_all"] = st.n

        def emit_ar_begin(slots, tag):
            k = len(slots)
            G[("ar_k", tag)] = k

            def ar_tr(e, slots=slots):
                e.wait_ge(ve.h, ve.n)
                e.wait_ge(sc.h, sc.n)
                for i, c in enumerate(slots):
                    ins = e.transpose(PS2[0:1, 128 * i:128 * (i + 1)],
                                      FMAX[:, c:c + 1], IDENT[:])
                ins.then_inc(mm.h, 1)
            block.tensor(ar_tr)
            mm.n += 1
            mm_t = mm.n

            def ar_ev(e, k=k, mm_t=mm_t):
                e.wait_ge(mm.h, mm_t)
                e.copy(out=CCS[0:1, 0:128 * k], in_=PS2[0:1, 0:128 * k])
                sgap(e)
                e.copy(out=JUNK[:, 0:128], in_=JUNK[:, 0:128]).then_inc(sc.h, 1)
            block.scalar(ar_ev)
            sc.n += 1
            sc_e = sc.n

            def ar_red(e, k=k, sc_e=sc_e):
                e.wait_ge(sc.h, sc_e)
                e.tensor_reduce(out=CCT[0:1, 0:k].unsqueeze(2),
                                in_=CCS[0:1, 0:128 * k].rearrange("p (r c) -> p r c", c=128),
                                axis=mybir.AxisListType.X, op=OP.max)
                vgap(e)
                e.tensor_scalar(out=CCT[0:1, 4:4 + k], in0=CCT[0:1, 0:k],
                                scalar1=1.0 / 32767.0, scalar2=1e-12,
                                op0=OP.mult, op1=OP.max)
                vgap(e)
                e.tensor_copy(out=JUNK[:, 0:128], in_=JUNK[:, 0:128]).then_inc(ve.h, 1)
            block.vector(ar_red)
            ve.n += 1
            ve_r = ve.n

            def ar_rc(e, k=k, ve_r=ve_r):
                e.wait_ge(ve.h, ve_r)
                e.activation(out=CCT[0:1, 8:8 + k], in_=CCT[0:1, 4:4 + k], func=AF.Ln)
                sgap(e)
                e.activation(out=CCT[0:1, 12:12 + k], in_=CCT[0:1, 8:8 + k], func=AF.Exp,
                             scale=-1.0)
                sgap(e)
                e.copy(out=JUNK[:, 0:128], in_=JUNK[:, 0:128]).then_inc(sc.h, 1)
            block.scalar(ar_rc)
            sc.n += 1
            sc_r = sc.n

            def ar_pk(e, k=k, sc_r=sc_r):
                e.wait_ge(sc.h, sc_r)
                e.memset(CCS[0:1, 0:256], 0.0)
                vgap(e)
                e.tensor_scalar(out=CCS[0:1, 0:k], in0=CCT[0:1, 12:12 + k],
                                scalar1=-1.0, scalar2=None, op0=OP.mult)
                vgap(e)
                e.tensor_copy(out=CCS[0:1, 4:4 + k], in_=CCT[0:1, 4:4 + k])
                vgap(e)
                e.tensor_copy(out=JUNK[:, 0:128], in_=JUNK[:, 0:128]).then_inc(ve.h, 1)
            block.vector(ar_pk)
            ve.n += 1
            ve_p = ve.n

            def ar_up(e, ve_p=ve_p):
                e.wait_ge(ve.h, ve_p)
                e.dma_start(out=cc_in[:], in_=CCS[:]).then_inc(st.h, 16)
            block.sync(ar_up)
            st.n += 16
            st_u = st.n

            def ar_cc(e, st_u=st_u):
                e.wait_ge(st.h, st_u)
                e.collective_compute(
                    "AllReduce", OP.max, ins=[cc_in[:]], outs=[cc_out[:]],
                    replica_groups=[[0, 1, 2, 3, 4, 5, 6, 7]],
                ).then_inc(gp.h, 1)
            block.gpsimd(ar_cc)
            gp.n += 1
            G[("ar_gp", tag)] = gp.n

        def emit_ar_end(dst, tag):
            k = G[("ar_k", tag)]
            gp_c = G[("ar_gp", tag)]

            def ar_br(e, k=k, gp_c=gp_c, dst=dst):
                e.wait_ge(gp.h, gp_c)
                e.dma_start(out=SCL[:, dst:dst + 8],
                            in_=cc_out[0:1, 0:8].broadcast_to((128, 8))).then_inc(gs.h, 16)
            block.gpsimd(ar_br)
            gs.n += 16
            gs_b = gs.n

            def ar_ng(e, k=k, st_b=gs_b, dst=dst):
                e.wait_ge(gs.h, st_b)
                e.tensor_scalar(out=SCL[:, dst:dst + k], in0=SCL[:, dst:dst + k],
                                scalar1=-1.0, scalar2=None, op0=OP.mult)
                vgap(e)
                e.tensor_copy(out=JUNK[:, 0:128], in_=JUNK[:, 0:128]).then_inc(ve.h, 1)
            block.vector(ar_ng)
            ve.n += 1
            G[("scl_ready", tag)] = ve.n

        def emit_allreduce(slots, dst, tag):
            emit_ar_begin(slots, tag)
            emit_ar_end(dst, tag)

        def qdq3(e, src, tmp1, tmp2, dstout, recip_col, s_col, lo=-32768.0, hi=32767.0):
            e.tensor_scalar(out=tmp1, in0=src, scalar1=recip_col, scalar2=MAGIC,
                            op0=OP.mult, op1=OP.add)
            e.tensor_scalar(out=tmp2, in0=tmp1, scalar1=MAGIC, scalar2=lo,
                            op0=OP.subtract, op1=OP.max)
            e.tensor_scalar(out=dstout, in0=tmp2, scalar1=hi, scalar2=s_col,
                            op0=OP.min, op1=OP.mult)

        def emit_ffn(fn, xin_t, xout_t, write_out):
            G[("ffn_begin_st", fn)] = st.n
            G[("ffn_begin_gs", fn)] = gs.n
            G[("ffn_begin_ve", fn)] = ve.n
            G[("ffn_begin_mm", fn)] = mm.n
            G[("ffn_begin_sc", fn)] = sc.n
            LNQ = VAUG

            for tt in range(16):
                xo = XROFF[tt % 2]

                def f_ld(e, tt=tt, xo=xo, fn=fn, xin_t=xin_t):
                    if tt == 0:
                        e.wait_ge(st.h, G[("ffn_begin_st", fn)])
                        e.wait_ge(gs.h, G[("ffn_begin_gs", fn)])
                        e.wait_ge(ve.h, G[("ffn_begin_ve", fn)])
                        e.wait_ge(mm.h, G[("ffn_begin_mm", fn)])
                        e.wait_ge(sc.h, G[("ffn_begin_sc", fn)])
                    if tt >= 2:
                        e.wait_ge(ve.h, G[("f_apply", fn, tt - 2)])
                    e.dma_start(out=PK[:, xo:xo + 1024], in_=xin_t[tt]).then_inc(ld.h, 16)
                block.sync(f_ld)
                ld.n += 16
                ld_x = ld.n

                def f_sum(e, xo=xo, ld_x=ld_x, tt=tt):
                    e.wait_ge(ld.h, ld_x + (16 if tt < 15 else 0))
                    e.tensor_scalar(out=JUNK2[:, 0:1024], in0=PK[:, xo:xo + 1024], scalar1=1.0,
                                    scalar2=None, op0=OP.mult, op1=OP.add,
                                    accum_out=STAT[:, 26:27])
                    vgap(e)
                    e.tensor_copy(out=JUNK[:, 0:128], in_=JUNK[:, 0:128]).then_inc(ve.h, 1)
                block.vector(f_sum)
                ve.n += 1

                def f_sq(e, xo=xo, ld_x=ld_x, tt=tt):
                    e.wait_ge(ld.h, ld_x + (16 if tt < 15 else 0))
                    e.activation(out=JUNK2[:, 1024:2048], in_=PK[:, xo:xo + 1024],
                                 func=AF.Square, accum_out=STAT[:, 27:28])
                    sgap(e)
                    e.copy(out=JUNK[:, 0:128], in_=JUNK[:, 0:128]).then_inc(sc.h, 1)
                block.scalar(f_sq)
                sc.n += 1
                sc_sq = sc.n

                def f_comb(e, sc_sq=sc_sq):
                    e.wait_ge(sc.h, sc_sq)
                    e.tensor_scalar(out=STAT[:, 24:25], in0=STAT[:, 26:27], scalar1=1.0 / 1024,
                                    scalar2=None, op0=OP.mult)
                    e.tensor_scalar(out=MCOL[:, 2:3], in0=STAT[:, 27:28], scalar1=1.0 / 1024,
                                    scalar2=None, op0=OP.mult)
                    vgap(e)
                    e.tensor_tensor(out=MCOL[:, 0:1], in0=STAT[:, 24:25], in1=STAT[:, 24:25],
                                    op=OP.mult)
                    vgap(e)
                    e.tensor_tensor(out=MCOL[:, 1:2], in0=MCOL[:, 2:3], in1=MCOL[:, 0:1],
                                    op=OP.subtract)
                    vgap(e)
                    e.tensor_scalar(out=MCOL[:, 1:2], in0=MCOL[:, 1:2], scalar1=LN_EPS,
                                    scalar2=None, op0=OP.add)
                    vgap(e)
                    e.tensor_copy(out=JUNK[:, 0:128], in_=JUNK[:, 0:128]).then_inc(ve.h, 1)
                block.vector(f_comb)
                ve.n += 1
                ve_c = ve.n

                def f_rstd(e, ve_c=ve_c):
                    e.wait_ge(ve.h, ve_c)
                    e.activation(out=MCOL[:, 3:4], in_=MCOL[:, 1:2], func=AF.Ln)
                    sgap(e)
                    e.activation(out=MCOL[:, 5:6], in_=MCOL[:, 3:4], func=AF.Exp, scale=-0.5)
                    sgap(e)
                    e.copy(out=JUNK[:, 0:128], in_=JUNK[:, 0:128]).then_inc(sc.h, 1)
                block.scalar(f_rstd)
                sc.n += 1
                sc_r = sc.n

                def f_apply(e, tt=tt, xo=xo, sc_r=sc_r, fn=fn):
                    e.wait_ge(sc.h, sc_r)
                    e.tensor_scalar(out=LNQ[:, 1024 * tt:1024 * (tt + 1)],
                                    in0=PK[:, xo:xo + 1024], scalar1=STAT[:, 24:25],
                                    scalar2=MCOL[:, 5:6], op0=OP.subtract, op1=OP.mult)
                    e.tensor_reduce(out=FMAX[:, tt:tt + 1],
                                    in_=LNQ[:, 1024 * tt:1024 * (tt + 1)],
                                    axis=mybir.AxisListType.X, op=OP.max,
                                    apply_absolute_value=True)
                    vgap(e)
                    e.tensor_copy(out=JUNK[:, 0:128], in_=JUNK[:, 0:128]).then_inc(ve.h, 1)
                block.vector(f_apply)
                ve.n += 1
                G[("f_apply", fn, tt)] = ve.n

            def f_m0red(e):
                e.wait_ge(ve.h, ve.n)
                e.tensor_reduce(out=FMAX[:, 62:63], in_=FMAX[:, 0:16],
                                axis=mybir.AxisListType.X, op=OP.max)
                vgap(e)
                e.tensor_copy(out=JUNK[:, 0:128], in_=JUNK[:, 0:128]).then_inc(ve.h, 1)
            block.vector(f_m0red)
            ve.n += 1
            emit_allreduce([62], 0, ("m0", fn))
            SCL_M0 = G[("scl_ready", ("m0", fn))]

            for tt in range(16):
                def f_q(e, tt=tt, fn=fn, SCL_M0=SCL_M0):
                    e.wait_ge(ve.h, SCL_M0)
                    if G.get(("f_qtr", fn, tt - 1)) is not None:
                        e.wait_ge(mm.h, G[("f_qtr", fn, tt - 1)])
                    src = LNQ[:, 1024 * tt:1024 * (tt + 1)]
                    qdq3(e, src, JUNK2[:, 4096:5120], JUNK2[:, 5120:6144],
                         JUNK2[:, 0:1024], SCL[:, 0:1], SCL[:, 4:5])
                    e.tensor_copy(out=JUNK[:, 0:128], in_=JUNK[:, 0:128]).then_inc(ve.h, 1)
                block.vector(f_q)
                ve.n += 1
                ve_q = ve.n

                def f_qtr(e, tt=tt, fn=fn, ve_q=ve_q, SCL_M0=SCL_M0):
                    e.wait_ge(ve.h, ve_q)
                    if tt == 0:
                        e.wait_ge(ve.h, SCL_M0)
                    if G.get(("f_qev", fn, tt - 1)) is not None:
                        e.wait_ge(sc.h, G[("f_qev", fn, tt - 1)])
                    for half in range(2):
                        ps = [PS2, PS3][half]
                        for j in range(4):
                            ct = 4 * half + j
                            ins = e.transpose(ps[:, 128 * j:128 * (j + 1)],
                                              JUNK2[:, 128 * ct:128 * (ct + 1)], IDENT[:])
                    ins.then_inc(mm.h, 1)
                block.tensor(f_qtr)
                mm.n += 1
                G[("f_qtr", fn, tt)] = mm.n
                mm_tr = mm.n

                def f_qev(e, tt=tt, fn=fn, mm_tr=mm_tr):
                    e.wait_ge(mm.h, mm_tr)
                    for half in range(2):
                        ps = [PS2, PS3][half]
                        dst = LNT[:].rearrange("p (ct t) -> p ct t", t=2048)[
                            :, 4 * half:4 * half + 4, 128 * tt:128 * (tt + 1)]
                        ins = e.copy(out=dst, in_=ps[:].rearrange("p (b t) -> p b t", t=128))
                    ins.then_inc(sc.h, 1)
                block.scalar(f_qev)
                sc.n += 1
                G[("f_qev", fn, tt)] = sc.n
            SC_XQT = sc.n

            EXPUP = EXP[:, 4096:6144]
            for which, wdq, outs_t, fslot, ebuf in (
                    (0, wgatedqT, gateTs_t, 17, PQ),
                    (1, wupdqT, upTs_t, 16, EXPUP)):
                if which == 1:
                    ST_GATE = gs.n
                    emit_ar_begin([17], ("m2", fn))
                for jt in range(24):
                    it = 24 * which + jt
                    wb = [WR0, WR1][it % 2]

                    def f_uld(e, wdq=wdq, jt=jt, it=it, wb=wb, fn=fn):
                        if G.get(("f_umm", fn, it - 2)) is not None:
                            e.wait_ge(mm.h, G[("f_umm", fn, it - 2)])
                        w4 = wdq.rearrange("(hc p) (q j) -> p hc q j", p=128, j=128)
                        e.dma_start(out=wb[:, 0:1024].rearrange("p (hc j) -> p hc j", j=128),
                                    in_=w4[:, :, jt, :]).then_inc(ld.h, 16)
                    block.sync(f_uld)
                    ld.n += 16
                    ld_w = ld.n

                    for tc in range(4):
                        ps = PSS[(4 * it + tc) % 2]

                        def f_umm(e, tc=tc, ps=ps, it=it, ld_w=ld_w, fn=fn, wb=wb,
                                  jt=jt):
                            e.wait_ge(ld.h, ld_w + (16 if jt < 23 else 0))
                            if it == 0 and tc == 0:
                                e.wait_ge(sc.h, SC_XQT)
                                e.wait_ge(st.h, G["wdq_all"])
                            gk = ("f_uev", fn, 4 * it + tc - 2)
                            if G.get(gk) is not None:
                                e.wait_ge(sc.h, G[gk])
                            for hc in range(8):
                                ins = e.matmul(
                                    ps[:], wb[:, 128 * hc:128 * (hc + 1)],
                                    LNT[:, 2048 * hc + 512 * tc:2048 * hc + 512 * (tc + 1)],
                                    start=(hc == 0), stop=(hc == 7))
                            ins.then_inc(mm.h, 1)
                        block.tensor(f_umm)
                        mm.n += 1
                        G[("f_umm", fn, it)] = mm.n
                        mm_u = mm.n

                        def f_uev(e, tc=tc, ps=ps, it=it, mm_u=mm_u, fn=fn, ebuf=ebuf):
                            e.wait_ge(mm.h, mm_u)
                            if tc == 0 and G.get(("f_usp", fn, it - 1)) is not None:
                                e.wait_ge(gs.h, G[("f_usp", fn, it - 1)])
                            e.copy(out=ebuf[:, 512 * tc:512 * (tc + 1)],
                                   in_=ps[:]).then_inc(sc.h, 1)
                        block.scalar(f_uev)
                        sc.n += 1
                        G[("f_uev", fn, 4 * it + tc)] = sc.n
                    sc_tile = sc.n

                    def f_umax(e, jt=jt, fslot=fslot, sc_tile=sc_tile, ebuf=ebuf):
                        e.wait_ge(sc.h, sc_tile)
                        e.tensor_reduce(out=MCOL[:, 6:7], in_=ebuf[:],
                                        axis=mybir.AxisListType.X, op=OP.max,
                                        apply_absolute_value=True)
                        vgap(e)
                        if jt == 0:
                            e.tensor_copy(out=FMAX[:, fslot:fslot + 1], in_=MCOL[:, 6:7])
                        else:
                            e.tensor_tensor(out=FMAX[:, fslot:fslot + 1],
                                            in0=FMAX[:, fslot:fslot + 1], in1=MCOL[:, 6:7],
                                            op=OP.max)
                        vgap(e)
                        e.tensor_copy(out=JUNK[:, 0:128], in_=JUNK[:, 0:128]).then_inc(ve.h, 1)
                    block.vector(f_umax)
                    ve.n += 1
                    ve_mx = ve.n

                    def f_usp(e, jt=jt, outs_t=outs_t, ve_mx=ve_mx, ebuf=ebuf):
                        e.wait_ge(ve.h, ve_mx)
                        e.dma_start(out=outs_t[jt], in_=ebuf[:]).then_inc(gs.h, 16)
                    block.gpsimd(f_usp)
                    gs.n += 16
                    G[("f_usp", fn, it)] = gs.n
            ST_UPGATE = gs.n

            emit_ar_end(9, ("m2", fn))
            SCL_M12 = G[("scl_ready", ("m2", fn))]

            for jt in range(24):
                def p2_ld(e, jt=jt, fn=fn, ST_GATE=ST_GATE):
                    e.wait_ge(gs.h, ST_GATE)
                    if G.get(("p2_sp", fn, jt - 1)) is not None:
                        e.wait_ge(gs.h, G[("p2_sp", fn, jt - 1)])
                    if G.get(("p2_g2", fn, jt - 1)) is not None:
                        e.wait_ge(ve.h, G[("p2_g2", fn, jt - 1)])
                    e.dma_start(out=PQ[:], in_=gateTs_t[jt]).then_inc(ld.h, 16)
                block.sync(p2_ld)
                ld.n += 16
                ld_g = ld.n

                def p2_gq(e, ld_g=ld_g, SCL_M12=SCL_M12):
                    e.wait_ge(ld.h, ld_g)
                    e.wait_ge(ve.h, SCL_M12)
                    qdq3(e, PQ[:], JUNK2[:, 0:2048], JUNK2[:, 2048:4096], PK[:],
                         SCL[:, 9:10], SCL[:, 13:14])
                    e.tensor_copy(out=JUNK[:, 0:128], in_=JUNK[:, 0:128]).then_inc(ve.h, 1)
                block.vector(p2_gq)
                ve.n += 1
                ve_gq = ve.n

                def p2_sig(e, ve_gq=ve_gq):
                    e.wait_ge(ve.h, ve_gq)
                    e.activation(out=PA[:], in_=PK[:], func=AF.Sigmoid).then_inc(sc.h, 1)
                block.scalar(p2_sig)
                sc.n += 1
                sc_sig = sc.n

                def p2_g2(e, jt=jt, sc_sig=sc_sig, fn=fn):
                    e.wait_ge(sc.h, sc_sig)
                    e.tensor_scalar(out=JUNK2[:, 0:2048], in0=PA[:], scalar1=65536.0,
                                    scalar2=MAGIC, op0=OP.mult, op1=OP.add)
                    e.tensor_scalar(out=JUNK2[:, 2048:4096], in0=JUNK2[:, 0:2048],
                                    scalar1=MAGIC, scalar2=0.0, op0=OP.subtract, op1=OP.max)
                    e.tensor_scalar(out=PA[:], in0=JUNK2[:, 2048:4096], scalar1=65535.0,
                                    scalar2=1.0 / 65536.0, op0=OP.min, op1=OP.mult)
                    e.tensor_tensor(out=PQ[:], in0=PK[:], in1=PA[:], op=OP.mult)
                    e.tensor_reduce(out=MCOL[:, 6:7], in_=PQ[:], axis=mybir.AxisListType.X,
                                    op=OP.max, apply_absolute_value=True)
                    vgap(e)
                    if jt == 0:
                        e.tensor_copy(out=FMAX[:, 63:64], in_=MCOL[:, 6:7])
                    else:
                        e.tensor_tensor(out=FMAX[:, 63:64], in0=FMAX[:, 63:64],
                                        in1=MCOL[:, 6:7], op=OP.max)
                    vgap(e)
                    e.tensor_copy(out=JUNK[:, 0:128], in_=JUNK[:, 0:128]).then_inc(ve.h, 1)
                block.vector(p2_g2)
                ve.n += 1
                G[("p2_g2", fn, jt)] = ve.n
                ve_g2 = ve.n

                def p2_sp(e, jt=jt, ve_g2=ve_g2, fn=fn):
                    e.wait_ge(ve.h, ve_g2)
                    e.dma_start(out=g2Ts_t[jt], in_=PQ[:]).then_inc(gs.h, 16)
                block.gpsimd(p2_sp)
                gs.n += 16
                G[("p2_sp", fn, jt)] = gs.n
            ST_G2 = gs.n

            emit_allreduce([16, 63], 8, ("m13", fn))
            SCL_M3 = G[("scl_ready", ("m13", fn))]

            for jt in range(24):
                def p3_ld(e, jt=jt, fn=fn):
                    e.wait_ge(gs.h, ST_G2)
                    if G.get(("p3_sp", fn, jt - 1)) is not None:
                        e.wait_ge(gs.h, G[("p3_sp", fn, jt - 1)])
                    if G.get(("p3_q", fn, jt - 1)) is not None:
                        e.wait_ge(ve.h, G[("p3_q", fn, jt - 1)])
                    e.dma_start(out=PQ[:], in_=g2Ts_t[jt]).then_inc(ld.h, 16)
                    e.dma_start(out=PK[:], in_=upTs_t[jt]).then_inc(ld.h, 16)
                block.sync(p3_ld)
                ld.n += 32
                ld_g2 = ld.n

                def p3_q(e, jt=jt, ld_g2=ld_g2, SCL_M3=SCL_M3, fn=fn):
                    e.wait_ge(ld.h, ld_g2)
                    e.wait_ge(ve.h, SCL_M3)
                    qdq3(e, PQ[:], JUNK2[:, 0:2048], JUNK2[:, 2048:4096], PA[:],
                         SCL[:, 9:10], SCL[:, 13:14])
                    qdq3(e, PK[:], JUNK2[:, 0:2048], JUNK2[:, 2048:4096], PQ[:],
                         SCL[:, 8:9], SCL[:, 12:13])
                    e.tensor_tensor(out=PK[:], in0=PA[:], in1=PQ[:], op=OP.mult)
                    e.tensor_reduce(out=MCOL[:, 6:7], in_=PK[:], axis=mybir.AxisListType.X,
                                    op=OP.max, apply_absolute_value=True)
                    vgap(e)
                    if jt == 0:
                        e.tensor_copy(out=FMAX[:, 62:63], in_=MCOL[:, 6:7])
                    else:
                        e.tensor_tensor(out=FMAX[:, 62:63], in0=FMAX[:, 62:63],
                                        in1=MCOL[:, 6:7], op=OP.max)
                    vgap(e)
                    e.tensor_copy(out=JUNK[:, 0:128], in_=JUNK[:, 0:128]).then_inc(ve.h, 1)
                block.vector(p3_q)
                ve.n += 1
                G[("p3_q", fn, jt)] = ve.n
                ve_o = ve.n

                def p3_sp(e, jt=jt, ve_o=ve_o, fn=fn):
                    e.wait_ge(ve.h, ve_o)
                    e.dma_start(out=opreTs_t[jt], in_=PK[:]).then_inc(gs.h, 16)
                block.gpsimd(p3_sp)
                gs.n += 16
                G[("p3_sp", fn, jt)] = gs.n
            ST_OPRE = gs.n

            emit_allreduce([62], 24, ("m4", fn))
            SCL_M4 = G[("scl_ready", ("m4", fn))]

            def p4_wld(e, fn=fn):
                e.wait_ge(mm.h, mm.n)
                e.wait_ge(ve.h, ve.n)
                e.wait_ge(sc.h, sc.n)
                w4 = wdowndqT.rearrange("(n p) h -> n p h", p=128)
                for jc in range(24):
                    dst = (LNT[:, 1024 * jc:1024 * (jc + 1)] if jc < 16
                           else EXP[:, 1024 * (jc - 16):1024 * (jc - 15)])
                    e.dma_start(out=dst, in_=w4[jc]).then_inc(ld.h, 16)
            block.sync(p4_wld)
            ld.n += 16 * 24
            LD_WD = ld.n

            for tt in range(16):
                so = 4096 * (tt % 2)
                po = 1024 * (tt % 2)

                def p4_ld(e, tt=tt, so=so, po=po, fn=fn, xin_t=xin_t):
                    e.wait_ge(gs.h, ST_OPRE)
                    if G.get(("p4_st", fn, tt - 2)) is not None:
                        e.wait_ge(gs.h, G[("p4_st", fn, tt - 2)])
                    if G.get(("p4_ev", fn, tt - 2)) is not None:
                        e.wait_ge(ve.h, G[("p4_ev", fn, tt - 2)])
                    op4 = opreTs.rearrange("(q p) (u j) -> p q u j", p=128, j=128)
                    e.dma_start(out=VAUG[:, so:so + 3072].rearrange("p (q j) -> p q j", j=128),
                                in_=op4[:, :, tt, :]).then_inc(ld.h, 16)
                    e.dma_start(out=PA[:, po:po + 1024], in_=xin_t[tt]).then_inc(ld.h, 16)
                block.sync(p4_ld)
                ld.n += 32
                ld_o = ld.n

                def p4_q(e, so=so, ld_o=ld_o, SCL_M4=SCL_M4, tt=tt):
                    e.wait_ge(ld.h, ld_o + (32 if tt < 15 else 0))
                    e.wait_ge(ve.h, SCL_M4)
                    src = VAUG[:, so:so + 3072]
                    qdq3(e, src, VAUG[:, 11264:14336], VAUG[:, 8192:11264], src,
                         SCL[:, 24:25], SCL[:, 28:29])
                    e.tensor_copy(out=JUNK[:, 0:128], in_=JUNK[:, 0:128]).then_inc(ve.h, 1)
                block.vector(p4_q)
                ve.n += 1
                ve_oq = ve.n

                for hc in range(2):
                    ps = PSS[hc]

                    def p4_mm(e, tt=tt, so=so, hc=hc, ps=ps, ve_oq=ve_oq, fn=fn,
                              ld_o=ld_o):
                        e.wait_ge(ve.h, ve_oq)
                        if tt == 0 and hc == 0:
                            e.wait_ge(ld.h, ld_o + 32)
                        gk = ("p4_ev1", fn, 2 * (tt - 1) + hc)
                        if G.get(gk) is not None:
                            e.wait_ge(ve.h, G[gk])
                        for jc in range(24):
                            wsrc = (LNT[:, 1024 * jc + 512 * hc:1024 * jc + 512 * (hc + 1)]
                                    if jc < 16 else
                                    EXP[:, 1024 * (jc - 16) + 512 * hc:
                                        1024 * (jc - 16) + 512 * (hc + 1)])
                            ins = e.matmul(ps[:], VAUG[:, so + 128 * jc:so + 128 * (jc + 1)],
                                           wsrc, start=(jc == 0), stop=(jc == 23))
                        ins.then_inc(mm.h, 1)
                    block.tensor(p4_mm)
                    mm.n += 1
                    mm_d = mm.n

                    def p4_ev(e, hc=hc, ps=ps, po=po, mm_d=mm_d):
                        e.wait_ge(mm.h, mm_d)
                        e.tensor_tensor(out=PA[:, po + 512 * hc:po + 512 * (hc + 1)],
                                        in0=ps[:],
                                        in1=PA[:, po + 512 * hc:po + 512 * (hc + 1)],
                                        op=OP.add).then_inc(ve.h, 1)
                    block.vector(p4_ev)
                    ve.n += 1
                    G[("p4_ev1", fn, 2 * tt + hc)] = ve.n
                G[("p4_ev", fn, tt)] = ve.n
                ve_x = ve.n

                def p4_st(e, tt=tt, po=po, ve_x=ve_x, fn=fn, write_out=write_out,
                          xout_t=xout_t):
                    e.wait_ge(ve.h, ve_x)
                    e.dma_start(out=xout_t[tt], in_=PA[:, po:po + 1024]).then_inc(gs.h, 16)
                    if write_out:
                        e.dma_start(out=out_t[tt], in_=PA[:, po:po + 1024]).then_inc(gs.h, 16)
                block.gpsimd(p4_st)
                gs.n += 32 if write_out else 16
                G[("p4_st", fn, tt)] = gs.n

        if phase in ("x2", "x3", "full", "dbgb", "fullx3", "fullx2"):
            emit_wdq()
            emit_ffn(1, x1s_t, x2s_t, write_out=(phase in ("x2", "fullx2")))

        NEG = -1e38
        LNEPS_WKV = math.log(1e-8)

        def emit_attnb(xin_t, xout_t, write_out):
            G["b_begin_st"] = st.n
            G["b_begin_gs"] = gs.n
            G["b_begin_ve"] = ve.n
            G["b_begin_mm"] = mm.n
            G["b_begin_sc"] = sc.n

            for tt in range(16):
                xo = XROFF[tt % 2]

                def b0_ld(e, tt=tt, xo=xo):
                    if tt == 0:
                        e.wait_ge(st.h, G["b_begin_st"])
                        e.wait_ge(gs.h, G["b_begin_gs"])
                        e.wait_ge(ve.h, G["b_begin_ve"])
                        e.wait_ge(mm.h, G["b_begin_mm"])
                        e.wait_ge(sc.h, G["b_begin_sc"])
                    if tt >= 2:
                        e.wait_ge(ve.h, G[("b0_ap2", tt - 2)])
                    e.dma_start(out=PK[:, xo:xo + 1024], in_=xin_t[tt]).then_inc(ld.h, 16)
                block.sync(b0_ld)
                ld.n += 16
                ld_x = ld.n

                def b0_sum(e, xo=xo, ld_x=ld_x):
                    e.wait_ge(ld.h, ld_x)
                    e.tensor_scalar(out=JUNK2[:, 0:1024], in0=PK[:, xo:xo + 1024], scalar1=1.0,
                                    scalar2=None, op0=OP.mult, op1=OP.add,
                                    accum_out=STAT[:, 26:27])
                    vgap(e)
                    e.tensor_copy(out=JUNK[:, 0:128], in_=JUNK[:, 0:128]).then_inc(ve.h, 1)
                block.vector(b0_sum)
                ve.n += 1

                def b0_sq(e, xo=xo, ld_x=ld_x):
                    e.wait_ge(ld.h, ld_x)
                    e.activation(out=JUNK2[:, 1024:2048], in_=PK[:, xo:xo + 1024],
                                 func=AF.Square, accum_out=STAT[:, 27:28])
                    sgap(e)
                    e.copy(out=JUNK[:, 0:128], in_=JUNK[:, 0:128]).then_inc(sc.h, 1)
                block.scalar(b0_sq)
                sc.n += 1
                sc_sq = sc.n

                def b0_comb(e, sc_sq=sc_sq):
                    e.wait_ge(sc.h, sc_sq)
                    e.tensor_scalar(out=STAT[:, 24:25], in0=STAT[:, 26:27], scalar1=1.0 / 1024,
                                    scalar2=None, op0=OP.mult)
                    e.tensor_scalar(out=MCOL[:, 2:3], in0=STAT[:, 27:28], scalar1=1.0 / 1024,
                                    scalar2=None, op0=OP.mult)
                    vgap(e)
                    e.tensor_tensor(out=MCOL[:, 0:1], in0=STAT[:, 24:25], in1=STAT[:, 24:25],
                                    op=OP.mult)
                    vgap(e)
                    e.tensor_tensor(out=MCOL[:, 1:2], in0=MCOL[:, 2:3], in1=MCOL[:, 0:1],
                                    op=OP.subtract)
                    vgap(e)
                    e.tensor_scalar(out=MCOL[:, 1:2], in0=MCOL[:, 1:2], scalar1=LN_EPS,
                                    scalar2=None, op0=OP.add)
                    vgap(e)
                    e.tensor_copy(out=JUNK[:, 0:128], in_=JUNK[:, 0:128]).then_inc(ve.h, 1)
                block.vector(b0_comb)
                ve.n += 1
                ve_c = ve.n

                def b0_rstd(e, ve_c=ve_c):
                    e.wait_ge(ve.h, ve_c)
                    e.activation(out=MCOL[:, 3:4], in_=MCOL[:, 1:2], func=AF.Ln)
                    sgap(e)
                    e.activation(out=MCOL[:, 5:6], in_=MCOL[:, 3:4], func=AF.Exp, scale=-0.5)
                    sgap(e)
                    e.copy(out=JUNK[:, 0:128], in_=JUNK[:, 0:128]).then_inc(sc.h, 1)
                block.scalar(b0_rstd)
                sc.n += 1
                sc_r = sc.n

                def b0_ap1(e, tt=tt, xo=xo, sc_r=sc_r):
                    e.wait_ge(sc.h, sc_r)
                    if tt >= 1:
                        e.wait_ge(mm.h, G[("b0_tr", tt - 1)])
                    e.tensor_scalar(out=PQ[:, 0:1024], in0=PK[:, xo:xo + 1024],
                                    scalar1=STAT[:, 24:25], scalar2=MCOL[:, 5:6],
                                    op0=OP.subtract, op1=OP.mult)
                    e.tensor_scalar(out=JUNK2[:, 0:1024], in0=PQ[:, 0:1024], scalar1=1.0,
                                    scalar2=None, op0=OP.mult, op1=OP.add,
                                    accum_out=STAT[:, 26:27])
                    vgap(e)
                    e.tensor_copy(out=JUNK[:, 0:128], in_=JUNK[:, 0:128]).then_inc(ve.h, 1)
                block.vector(b0_ap1)
                ve.n += 1
                ve_a1 = ve.n

                def b0_sq2(e, ve_a1=ve_a1):
                    e.wait_ge(ve.h, ve_a1)
                    e.activation(out=JUNK2[:, 1024:2048], in_=PQ[:, 0:1024],
                                 func=AF.Square, accum_out=STAT[:, 27:28])
                    sgap(e)
                    e.copy(out=JUNK[:, 0:128], in_=JUNK[:, 0:128]).then_inc(sc.h, 1)
                block.scalar(b0_sq2)
                sc.n += 1
                sc_sq2 = sc.n

                def b0_comb2(e, sc_sq2=sc_sq2):
                    e.wait_ge(sc.h, sc_sq2)
                    e.tensor_scalar(out=STAT[:, 24:25], in0=STAT[:, 26:27], scalar1=1.0 / 1024,
                                    scalar2=None, op0=OP.mult)
                    e.tensor_scalar(out=MCOL[:, 2:3], in0=STAT[:, 27:28], scalar1=1.0 / 1024,
                                    scalar2=None, op0=OP.mult)
                    vgap(e)
                    e.tensor_tensor(out=MCOL[:, 0:1], in0=STAT[:, 24:25], in1=STAT[:, 24:25],
                                    op=OP.mult)
                    vgap(e)
                    e.tensor_tensor(out=MCOL[:, 1:2], in0=MCOL[:, 2:3], in1=MCOL[:, 0:1],
                                    op=OP.subtract)
                    vgap(e)
                    e.tensor_scalar(out=MCOL[:, 1:2], in0=MCOL[:, 1:2], scalar1=LN_EPS,
                                    scalar2=None, op0=OP.add)
                    vgap(e)
                    e.tensor_copy(out=JUNK[:, 0:128], in_=JUNK[:, 0:128]).then_inc(ve.h, 1)
                block.vector(b0_comb2)
                ve.n += 1
                ve_c2 = ve.n

                def b0_rstd2(e, ve_c2=ve_c2):
                    e.wait_ge(ve.h, ve_c2)
                    e.activation(out=MCOL[:, 3:4], in_=MCOL[:, 1:2], func=AF.Ln)
                    sgap(e)
                    e.activation(out=MCOL[:, 6:7], in_=MCOL[:, 3:4], func=AF.Exp, scale=-0.5)
                    sgap(e)
                    e.copy(out=JUNK[:, 0:128], in_=JUNK[:, 0:128]).then_inc(sc.h, 1)
                block.scalar(b0_rstd2)
                sc.n += 1
                sc_r2 = sc.n

                def b0_ap2(e, tt=tt, sc_r2=sc_r2):
                    e.wait_ge(sc.h, sc_r2)
                    e.tensor_scalar(out=PA[:, 0:1024], in0=PQ[:, 0:1024],
                                    scalar1=STAT[:, 24:25], scalar2=MCOL[:, 6:7],
                                    op0=OP.subtract, op1=OP.mult)
                    vgap(e)
                    e.tensor_copy(out=JUNK[:, 0:128], in_=JUNK[:, 0:128]).then_inc(ve.h, 1)
                block.vector(b0_ap2)
                ve.n += 1
                G[("b0_ap2", tt)] = ve.n
                ve_a2 = ve.n

                def b0_tr(e, tt=tt, ve_a2=ve_a2):
                    e.wait_ge(ve.h, ve_a2)
                    if tt >= 1:
                        e.wait_ge(sc.h, G[("b0_ev", tt - 1)])
                        e.wait_ge(st.h, G[("b0_sp", tt - 1)])
                    for half in range(2):
                        ps = [PS2, PS3][half]
                        for j in range(4):
                            ct = 4 * half + j
                            e.transpose(ps[:, 128 * j:128 * (j + 1)],
                                        PQ[:, 128 * ct:128 * (ct + 1)], IDENT[:])
                    for half in range(2):
                        ps = [PS0, PS1][half]
                        for j in range(4):
                            ct = 4 * half + j
                            ins = e.transpose(ps[:, 128 * j:128 * (j + 1)],
                                              PA[:, 128 * ct:128 * (ct + 1)], IDENT[:])
                    ins.then_inc(mm.h, 1)
                block.tensor(b0_tr)
                mm.n += 1
                G[("b0_tr", tt)] = mm.n
                mm_tr = mm.n

                def b0_ev(e, tt=tt, mm_tr=mm_tr):
                    e.wait_ge(mm.h, mm_tr)
                    for half in range(2):
                        ps = [PS2, PS3][half]
                        dst = PA[:, 1024:2048].rearrange("p (b u) -> p b u", u=128)[
                            :, 4 * half:4 * half + 4, :]
                        e.copy(out=dst, in_=ps[:].rearrange("p (b u) -> p b u", u=128))
                    for half in range(2):
                        ps = [PS0, PS1][half]
                        dst = LNT[:].rearrange("p (ct t) -> p ct t", t=2048)[
                            :, 4 * half:4 * half + 4, 128 * tt:128 * (tt + 1)]
                        ins = e.copy(out=dst, in_=ps[:].rearrange("p (b t) -> p b t", t=128))
                    ins.then_inc(sc.h, 1)
                block.scalar(b0_ev)
                sc.n += 1
                G[("b0_ev", tt)] = sc.n
                sc_ev = sc.n

                def b0_g1(e, tt=tt, sc_ev=sc_ev):
                    e.wait_ge(sc.h, sc_ev)
                    for ct in range(8):
                        ins = e.tensor_scalar(
                            out=LNT[:, 2048 * ct + 128 * tt:2048 * ct + 128 * (tt + 1)],
                            in0=LNT[:, 2048 * ct + 128 * tt:2048 * ct + 128 * (tt + 1)],
                            scalar1=PRM[:, 0 + ct:1 + ct], scalar2=PRM[:, 8 + ct:9 + ct],
                            op0=OP.mult, op1=OP.add)
                    ins.then_inc(ve.h, 1)
                block.vector(b0_g1)
                ve.n += 1
                ve_g1 = ve.n

                def b0_sp(e, tt=tt, sc_ev=sc_ev):
                    e.wait_ge(sc.h, sc_ev)
                    xb4 = xb0s.rearrange("(ct p) (q j) -> p ct q j", p=128, j=128)
                    e.dma_start(out=xb4[:, :, tt, :],
                                in_=PA[:, 1024:2048].rearrange("p (b u) -> p b u", u=128)
                                ).then_inc(st.h, 16)
                block.sync(b0_sp)
                st.n += 16
                G[("b0_sp", tt)] = st.n
            SC_B0 = sc.n
            VE_B0 = ve.n
            ST_B0 = st.n

            def emit_b1_wld(jt):
                def b1_wld(e, jt=jt):
                    wb_off = 1024 * (jt % 2)
                    if G.get(("b1_mm", jt - 2)) is not None:
                        e.wait_ge(mm.h, G[("b1_mm", jt - 2)])
                    w4 = d["W_tm"].rearrange("(hc p) (q j) -> p hc q j", p=128, j=128)
                    e.dma_start(out=WR0[:, wb_off:wb_off + 1024].rearrange(
                        "p (hc j) -> p hc j", j=128), in_=w4[:, :, jt, :]).then_inc(ld.h, 16)
                    if G.get(("b1_mixdone", jt - 2)) is not None:
                        e.wait_ge(ve.h, G[("b1_mixdone", jt - 2)])
                    xb8 = xb0s.rearrange("(ct p) t -> ct p t", p=128)
                    e.dma_start(out=VAUG[:, 2048 * (jt % 2):2048 * (jt % 2) + 2048],
                                in_=xb8[jt % 8]).then_inc(ld.h, 16)
                block.sync(b1_wld)
                ld.n += 32
                G[("b1_ldw", jt)] = ld.n

            emit_b1_wld(0)
            for jt in range(24):
                ct8 = jt % 8
                kind = jt // 8
                wb_off = 1024 * (jt % 2)
                ld_w = G[("b1_ldw", jt)]

                for tc in range(4):
                    ps = PSS[tc % 2]

                    def b1_mm(e, tc=tc, ps=ps, jt=jt, ld_w=ld_w, wb_off=wb_off):
                        e.wait_ge(ld.h, ld_w + (32 if jt == 0 else 0))
                        if jt == 0 and tc == 0:
                            e.wait_ge(ve.h, VE_B0)
                            e.wait_ge(st.h, ST_B0)
                        gk = ("b1_ev", 4 * jt + tc - 2)
                        if G.get(gk) is not None:
                            e.wait_ge(sc.h, G[gk])
                        for hc in range(8):
                            ins = e.matmul(
                                ps[:], WR0[:, wb_off + 128 * hc:wb_off + 128 * (hc + 1)],
                                LNT[:, 2048 * hc + 512 * tc:2048 * hc + 512 * (tc + 1)],
                                start=(hc == 0), stop=(hc == 7))
                        ins.then_inc(mm.h, 1)
                    block.tensor(b1_mm)
                    mm.n += 1
                    G[("b1_mm", jt)] = mm.n
                    mm_k = mm.n

                    def b1_ev(e, tc=tc, ps=ps, jt=jt, mm_k=mm_k):
                        e.wait_ge(mm.h, mm_k)
                        if tc == 0 and G.get(("b1_sp", jt - 1)) is not None:
                            e.wait_ge(st.h, G[("b1_sp", jt - 1)])
                        e.copy(out=PQ[:, 512 * tc:512 * (tc + 1)], in_=ps[:]).then_inc(sc.h, 1)
                    block.scalar(b1_ev)
                    sc.n += 1
                    G[("b1_ev", 4 * jt + tc)] = sc.n
                sc_t = sc.n
                if jt + 1 < 24:
                    emit_b1_wld(jt + 1)

                def b1_mix(e, jt=jt, ct8=ct8, kind=kind, sc_t=sc_t):
                    e.wait_ge(sc.h, sc_t)
                    xpo = 2048 * (jt % 2)
                    e.tensor_copy(out=PA[:, 1:2048], in_=VAUG[:, xpo:xpo + 2047])
                    e.tensor_copy(out=PA[:, 0:1], in_=VAUG[:, xpo:xpo + 1])
                    vgap(e)
                    tmcol = PRM[:, 48 + 8 * kind + ct8:49 + 8 * kind + ct8]
                    e.tensor_tensor(out=PK[:, 0:2048], in0=PQ[:], in1=PA[:, 0:2048],
                                    op=OP.subtract)
                    e.scalar_tensor_tensor(out=PQ[:], in0=PK[:, 0:2048], scalar=tmcol,
                                           in1=PA[:, 0:2048], op0=OP.mult, op1=OP.add)
                    vgap(e)
                    e.tensor_copy(out=JUNK[:, 0:128], in_=JUNK[:, 0:128]).then_inc(ve.h, 1)
                block.vector(b1_mix)
                ve.n += 1
                G[("b1_mixdone", jt)] = ve.n
                ve_m = ve.n

                if kind == 2:
                    def b1_sig(e, ve_m=ve_m):
                        e.wait_ge(ve.h, ve_m)
                        e.activation(out=PK[:, 0:2048], in_=PQ[:], func=AF.Sigmoid).then_inc(
                            sc.h, 1)
                    block.scalar(b1_sig)
                    sc.n += 1
                    sc_sig = sc.n

                    def b1_sp(e, jt=jt, ct8=ct8, sc_sig=sc_sig):
                        e.wait_ge(sc.h, sc_sig)
                        e.dma_start(out=rmixs.rearrange("(ct p) t -> ct p t", p=128)[ct8],
                                    in_=PK[:, 0:2048]).then_inc(st.h, 16)
                    block.sync(b1_sp)
                    st.n += 16
                else:
                    def b1_sp(e, jt=jt, ct8=ct8, kind=kind, ve_m=ve_m):
                        e.wait_ge(ve.h, ve_m)
                        tgt = kmixs if kind == 0 else vmixs
                        e.dma_start(out=tgt.rearrange("(ct p) t -> ct p t", p=128)[ct8],
                                    in_=PQ[:]).then_inc(st.h, 16)
                    block.sync(b1_sp)
                    st.n += 16
                G[("b1_sp", jt)] = st.n
            ST_B1 = st.n
            MM_B1 = mm.n

            if phase == "dbgb":
                def dbgb1(e):
                    e.wait_ge(st.h, ST_B1)
                    e.wait_ge(ve.h, ve.n)
                    e.dma_start(out=dbg_lng1[:], in_=LNT[:]).then_inc(st.h, 16)
                    for nmv, srcv in (("dbg_xb0", xb0s), ("dbg_k", kmixs), ("dbg_v", vmixs),
                                      ("dbg_r", rmixs)):
                        dd = {"dbg_xb0": dbg_xb0, "dbg_k": dbg_k, "dbg_v": dbg_v,
                              "dbg_r": dbg_r}[nmv]
                        for cti in range(8):
                            e.dma_start(out=dd.rearrange("(c p) t -> c p t", p=128)[cti],
                                        in_=srcv.rearrange("(c p) t -> c p t", p=128)[cti]
                                        ).then_inc(st.h, 16)
                block.sync(dbgb1)
                st.n += 16 * 33

            V0 = VAUG[:, 0:2048]; V1 = VAUG[:, 2048:4096]; V2 = VAUG[:, 4096:6144]
            V3 = VAUG[:, 6144:8192]; V4 = VAUG[:, 8192:10240]; V5 = VAUG[:, 10240:12288]
            V6 = VAUG[:, 12288:14336]; V7 = VAUG[:, 14336:16384]
            E0 = EXP[:, 0:2048]; E1 = EXP[:, 2048:4096]; E2 = EXP[:, 4096:6144]
            E3 = EXP[:, 6144:8192]
            FQ = PQ[:]
            CS3 = PA[:]
            IOT = PK[:]
            MX3 = WR0[:, 0:2048]

            def b2_pre(e):
                e.wait_ge(ve.h, ve.n)
                e.wait_ge(st.h, ST_B1)
                e.dma_start(out=PK[:], in_=d["iota"]).then_inc(ld.h, 16)
            block.sync(b2_pre)
            ld.n += 16
            LD_IOTA = ld.n

            def b2_negw(e):
                e.wait_ge(ld.h, LD_IOTA)
                e.tensor_scalar(out=PRM[:, 88:96], in0=PRM[:, 32:40], scalar1=-1.0,
                                scalar2=None, op0=OP.mult)
                vgap(e)
                e.tensor_copy(out=JUNK[:, 0:128], in_=JUNK[:, 0:128]).then_inc(ve.h, 1)
            block.vector(b2_negw)
            ve.n += 1

            def vstep(fn):
                def body(e, fn=fn, wait_sc=sc.n):
                    e.wait_ge(sc.h, wait_sc)
                    fn(e)
                    e.tensor_copy(out=JUNK[:, 0:128], in_=JUNK[:, 0:128]).then_inc(ve.h, 1)
                block.vector(body)
                ve.n += 1

            def sstep(fn):
                def body(e, fn=fn, wait_ve=ve.n):
                    e.wait_ge(ve.h, wait_ve)
                    fn(e)
                    e.copy(out=JUNK[:, 0:128], in_=JUNK[:, 0:128]).then_inc(sc.h, 1)
                block.scalar(body)
                sc.n += 1

            def chunkstats(src, base):
                def s1(e, src=src, base=base):
                    e.tensor_reduce(out=WR1[:, base:base + 64].unsqueeze(2),
                                    in_=src.rearrange("p (b u) -> p b u", u=32),
                                    axis=mybir.AxisListType.X, op=OP.max)
                    vgap(e)
                    e.tensor_copy(out=WR1[:, base + 64:base + 128], in_=WR1[:, base:base + 64])
                    vgap(e)
                    cur, oth = base + 64, base + 128
                    for shf in [1, 2, 4, 8, 16, 32]:
                        e.tensor_copy(out=WR1[:, oth:oth + shf], in_=WR1[:, cur:cur + shf])
                        e.tensor_tensor(out=WR1[:, oth + shf:oth + 64],
                                        in0=WR1[:, cur + shf:cur + 64],
                                        in1=WR1[:, cur:cur + 64 - shf], op=OP.max)
                        vgap(e)
                        cur, oth = oth, cur
                    e.tensor_tensor(out=WR1[:, base + 128 + 1:base + 128 + 64],
                                    in0=WR1[:, cur:cur + 63],
                                    in1=WR1[:, cur + 1:cur + 64], op=OP.subtract)
                    vgap(e)
                vstep(s1)

                def s2(e, base=base):
                    e.activation(out=WR1[:, base + 192 + 1:base + 192 + 64],
                                 in_=WR1[:, base + 128 + 1:base + 128 + 64], func=AF.Exp)
                    sgap(e)
                sstep(s2)

            def mexpand(dstwide, base):
                def s(e, dstwide=dstwide, base=base):
                    e.tensor_copy(
                        out=dstwide.rearrange("p (b u) -> p b u", u=32),
                        in_=WR1[:, base + 64:base + 128].unsqueeze(2).broadcast_to(
                            (128, 64, 32)))
                vstep(s)

            def spikes(data0, base):
                def s(e, data0=data0, base=base):
                    e.memset(data0, 1.0)
                    vgap(e)
                    e.tensor_copy(
                        out=data0.rearrange("p (b u) -> p b u", u=32)[:, 1:64, 0:1],
                        in_=WR1[:, base + 192 + 1:base + 192 + 64].unsqueeze(2))
                    vgap(e)
                vstep(s)

            for ct in range(8):
                wcol = PRM[:, 32 + ct:33 + ct]
                nwcol = PRM[:, 88 + ct:89 + ct]
                ucol = PRM[:, 40 + ct:41 + ct]

                def b2_ld(e, ct=ct):
                    if ct == 0:
                        e.wait_ge(ve.h, ve.n)
                        e.wait_ge(mm.h, MM_B1)
                    else:
                        e.wait_ge(ve.h, G[("b2_done", ct - 1)])
                    km = kmixs.rearrange("(c p) t -> c p t", p=128)
                    vm = vmixs.rearrange("(c p) t -> c p t", p=128)
                    rm = rmixs.rearrange("(c p) t -> c p t", p=128)
                    e.dma_start(out=V0, in_=km[ct]).then_inc(ld.h, 16)
                    e.dma_start(out=V1, in_=vm[ct]).then_inc(ld.h, 16)
                    e.dma_start(out=V2, in_=rm[ct]).then_inc(ld.h, 16)
                block.sync(b2_ld)
                ld.n += 48
                ld_kv = ld.n

                def s_g(e, ld_kv=ld_kv, nwcol=nwcol):
                    e.wait_ge(ld.h, ld_kv)
                    e.scalar_tensor_tensor(out=V3, in0=IOT, scalar=nwcol, in1=V0,
                                           op0=OP.mult, op1=OP.add)
                vstep(s_g)
                chunkstats(V3, 0)
                mexpand(E0, 0)
                def s_zs(e):
                    e.tensor_tensor(out=V4, in0=V3, in1=E0, op=OP.subtract)
                vstep(s_zs)
                def s_z(e):
                    e.activation(out=E1, in_=V4, func=AF.Exp)
                sstep(s_z)
                spikes(V5, 0)
                def s_scan1(e):
                    e.tensor_tensor_scan(out=V6, data0=V5, data1=E1, initial=0.0,
                                         op0=OP.mult, op1=OP.add)
                vstep(s_scan1)
                def s_wio(e):
                    e.tensor_tensor(out=E2, in0=V0, in1=V3, op=OP.subtract)
                vstep(s_wio)
                def s_lncs(e):
                    e.activation(out=V4, in_=V6, func=AF.Ln)
                sstep(s_lncs)
                def s_bb0(e):
                    e.tensor_tensor(out=V5, in0=E2, in1=E0, op=OP.add)
                    e.tensor_tensor(out=V7, in0=V5, in1=V4, op=OP.add)
                vstep(s_bb0)
                def s_shb(e):
                    e.tensor_copy(out=E3[:, 1:2048], in_=V7[:, 0:2047])
                    e.memset(E3[:, 0:1], NEG)
                    vgap(e)
                vstep(s_shb)
                def s_p20(e, wcol=wcol):
                    e.tensor_scalar(out=V5, in0=E3, scalar1=wcol, scalar2=None, op0=OP.add)
                    e.tensor_tensor(out=V4, in0=V5, in1=V0, op=OP.max)
                    e.tensor_tensor(out=V5, in0=V4, in1=E2, op=OP.subtract)
                vstep(s_p20)
                chunkstats(V5, 256)
                mexpand(E1, 256)
                def s_zs2(e):
                    e.tensor_tensor(out=V4, in0=V5, in1=E1, op=OP.subtract)
                vstep(s_zs2)
                def s_z2(e):
                    e.activation(out=V5, in_=V4, func=AF.Exp)
                sstep(s_z2)
                spikes(V4, 256)
                def s_scan2(e):
                    e.tensor_tensor_scan(out=V7, data0=V4, data1=V5, initial=0.0,
                                         op0=OP.mult, op1=OP.add)
                vstep(s_scan2)
                def s_f2(e):
                    e.tensor_tensor(out=V4, in0=E1, in1=E0, op=OP.subtract)
                    e.tensor_scalar(out=V5, in0=V4, scalar1=LNEPS_WKV, scalar2=None,
                                    op0=OP.add)
                vstep(s_f2)
                def s_f2e(e):
                    e.activation(out=V4, in_=V5, func=AF.Exp)
                sstep(s_f2e)
                def s_inner(e):
                    e.tensor_tensor(out=E3, in0=V4, in1=V7, op=OP.mult)
                    e.tensor_tensor(out=V5, in0=V6, in1=E3, op=OP.add)
                vstep(s_inner)
                def s_lnin(e):
                    e.activation(out=V4, in_=V5, func=AF.Ln)
                sstep(s_lnin)
                def s_bb(e):
                    e.tensor_tensor(out=V5, in0=E2, in1=E0, op=OP.add)
                    e.tensor_tensor(out=V7, in0=V5, in1=V4, op=OP.add)
                vstep(s_bb)
                def s_shb2(e):
                    e.tensor_copy(out=E3[:, 1:2048], in_=V7[:, 0:2047])
                    e.memset(E3[:, 0:1], NEG)
                    vgap(e)
                vstep(s_shb2)
                def s_p2f(e, wcol=wcol):
                    e.tensor_scalar(out=V5, in0=E3, scalar1=wcol, scalar2=None, op0=OP.add)
                    e.tensor_tensor(out=V4, in0=V5, in1=V0, op=OP.max)
                    e.tensor_tensor(out=V5, in0=V7, in1=V4, op=OP.subtract)
                    e.memset(V6, 1.0)
                vstep(s_p2f)
                def s_scanF(e):
                    e.tensor_tensor_scan(out=FQ, data0=V6, data1=V5, initial=0.0,
                                         op0=OP.mult, op1=OP.add)
                vstep(s_scanF)
                def s_shF(e):
                    e.tensor_copy(out=V5[:, 1:2048], in_=FQ[:, 0:2047])
                    e.memset(V5[:, 0:1], 0.0)
                    vgap(e)
                    e.tensor_tensor(out=V4, in0=V3, in1=V5, op=OP.subtract)
                vstep(s_shF)
                chunkstats(V4, 512)
                mexpand(MX3, 512)
                def s_zs3(e):
                    e.tensor_tensor(out=V6, in0=V4, in1=MX3, op=OP.subtract)
                vstep(s_zs3)
                def s_z3(e):
                    e.activation(out=E1, in_=V6, func=AF.Exp)
                sstep(s_z3)
                def s_zv(e):
                    e.tensor_tensor(out=V6, in0=E1, in1=V1, op=OP.mult)
                vstep(s_zv)
                spikes(V4, 512)
                def s_scan3(e):
                    e.tensor_tensor_scan(out=CS3, data0=V4, data1=V6, initial=0.0,
                                         op0=OP.mult, op1=OP.add)
                vstep(s_scan3)
                def s_B(e, ucol=ucol):
                    e.tensor_tensor(out=V4, in0=E2, in1=FQ, op=OP.add)
                    e.tensor_tensor(out=V5, in0=V4, in1=MX3, op=OP.add)
                    e.tensor_scalar(out=V4, in0=V0, scalar1=ucol, scalar2=None,
                                    op0=OP.add)
                    e.tensor_tensor(out=V6, in0=E3, in1=V4, op=OP.max)
                    e.tensor_tensor(out=E0[:, 1:2048], in0=V5[:, 0:2047],
                                    in1=V6[:, 1:2048], op=OP.subtract)
                    e.memset(E0[:, 0:1], NEG)
                vstep(s_B)
                def s_eA(e):
                    e.activation(out=E1, in_=E0, func=AF.Exp)
                sstep(s_eA)
                def s_t1(e):
                    e.tensor_tensor(out=E0[:, 1:2048], in0=E1[:, 1:2048],
                                    in1=CS3[:, 0:2047], op=OP.mult)
                    e.memset(E0[:, 0:1], 0.0)
                    vgap(e)
                    e.tensor_tensor(out=V4, in0=V4, in1=V6, op=OP.subtract) if False else None
                    e.tensor_tensor(out=V5, in0=V4, in1=V6, op=OP.subtract)
                vstep(s_t1)
                def s_e2(e):
                    e.activation(out=V4, in_=V5, func=AF.Exp)
                sstep(s_e2)
                def s_t2(e):
                    e.tensor_tensor(out=V5, in0=V4, in1=V1, op=OP.mult)
                    e.tensor_tensor(out=V3, in0=E3, in1=V6, op=OP.subtract)
                vstep(s_t2)
                def s_e1(e):
                    e.activation(out=E3, in_=V3, func=AF.Exp)
                sstep(s_e1)
                def s_den(e):
                    e.tensor_scalar(out=V6, in0=E3, scalar1=1e-8, scalar2=None, op0=OP.add)
                    e.tensor_tensor(out=V3, in0=V6, in1=V4, op=OP.add)
                vstep(s_den)
                def s_lnd(e):
                    e.activation(out=V6, in_=V3, func=AF.Ln)
                    sgap(e)
                    e.activation(out=V4, in_=V6, func=AF.Exp, scale=-1.0)
                sstep(s_lnd)
                def s_y(e, ct=ct):
                    e.tensor_tensor(out=V6, in0=E0, in1=V5, op=OP.add)
                    e.tensor_tensor(out=V5, in0=V6, in1=V4, op=OP.mult)
                    e.tensor_tensor(out=V6, in0=V5, in1=V2, op=OP.mult)
                vstep(s_y)
                def b2_xb0(e, ct=ct):
                    e.wait_ge(ve.h, ve.n)
                    xb8 = xb0s.rearrange("(c p) t -> c p t", p=128)
                    e.dma_start(out=V4, in_=xb8[ct]).then_inc(ld.h, 16)
                block.sync(b2_xb0)
                ld.n += 16
                ld_xb = ld.n
                def s_xb1(e, ct=ct, ld_xb=ld_xb):
                    e.wait_ge(ld.h, ld_xb)
                    e.tensor_tensor(out=LNT[:, 2048 * ct:2048 * (ct + 1)], in0=V4, in1=V6,
                                    op=OP.add)
                vstep(s_xb1)
                if phase == "dbgb" and ct == 0:
                    def dbgy(e):
                        e.wait_ge(ve.h, ve.n)
                        e.dma_start(out=dbg_y0[:], in_=V5).then_inc(st.h, 16)
                        e.dma_start(out=dbg_bb0[:], in_=V7).then_inc(st.h, 16)
                    block.sync(dbgy)
                    st.n += 32
                G[("b2_done", ct)] = ve.n
            VE_B2 = ve.n

            if phase in ("dbgb", "fullx3"):
                def dbgb2(e):
                    e.wait_ge(ve.h, VE_B2)
                    e.dma_start(out=dbg_xb1[:], in_=LNT[:]).then_inc(st.h, 16)
                block.sync(dbgb2)
                st.n += 16

            for tt in range(16):
                def b3_tr1(e, tt=tt):
                    if tt == 0:
                        e.wait_ge(ve.h, VE_B2)
                    if G.get(("b3_ev1", tt - 1)) is not None:
                        e.wait_ge(sc.h, G[("b3_ev1", tt - 1)])
                    for half in range(2):
                        ps = [PS2, PS3][half]
                        for j in range(4):
                            ct = 4 * half + j
                            ins = e.transpose(
                                ps[:, 128 * j:128 * (j + 1)],
                                LNT[:, 2048 * ct + 128 * tt:2048 * ct + 128 * (tt + 1)],
                                IDENT[:])
                    ins.then_inc(mm.h, 1)
                block.tensor(b3_tr1)
                mm.n += 1
                mm_t1 = mm.n

                def b3_ev1(e, tt=tt, mm_t1=mm_t1):
                    e.wait_ge(mm.h, mm_t1)
                    if G.get(("b3_ap", tt - 1)) is not None:
                        e.wait_ge(ve.h, G[("b3_ap", tt - 1)])
                    for half in range(2):
                        ps = [PS2, PS3][half]
                        dst = PK[:, 0:1024].rearrange("p (b u) -> p b u", u=128)[
                            :, 4 * half:4 * half + 4, :]
                        ins = e.copy(out=dst, in_=ps[:].rearrange("p (b u) -> p b u", u=128))
                    ins.then_inc(sc.h, 1)
                block.scalar(b3_ev1)
                sc.n += 1
                G[("b3_ev1", tt)] = sc.n
                sc_e1 = sc.n

                def b3_sum(e, sc_e1=sc_e1):
                    e.wait_ge(sc.h, sc_e1)
                    e.tensor_scalar(out=JUNK2[:, 0:1024], in0=PK[:, 0:1024], scalar1=1.0,
                                    scalar2=None, op0=OP.mult, op1=OP.add,
                                    accum_out=STAT[:, 26:27])
                    vgap(e)
                    e.tensor_copy(out=JUNK[:, 0:128], in_=JUNK[:, 0:128]).then_inc(ve.h, 1)
                block.vector(b3_sum)
                ve.n += 1
                ve_s1 = ve.n

                def b3_sq(e, ve_s1=ve_s1):
                    e.wait_ge(ve.h, ve_s1)
                    e.activation(out=JUNK2[:, 1024:2048], in_=PK[:, 0:1024],
                                 func=AF.Square, accum_out=STAT[:, 27:28])
                    sgap(e)
                    e.copy(out=JUNK[:, 0:128], in_=JUNK[:, 0:128]).then_inc(sc.h, 1)
                block.scalar(b3_sq)
                sc.n += 1
                sc_sq = sc.n

                def b3_comb(e, sc_sq=sc_sq):
                    e.wait_ge(sc.h, sc_sq)
                    e.tensor_scalar(out=STAT[:, 24:25], in0=STAT[:, 26:27], scalar1=1.0 / 1024,
                                    scalar2=None, op0=OP.mult)
                    e.tensor_scalar(out=MCOL[:, 2:3], in0=STAT[:, 27:28], scalar1=1.0 / 1024,
                                    scalar2=None, op0=OP.mult)
                    vgap(e)
                    e.tensor_tensor(out=MCOL[:, 0:1], in0=STAT[:, 24:25], in1=STAT[:, 24:25],
                                    op=OP.mult)
                    vgap(e)
                    e.tensor_tensor(out=MCOL[:, 1:2], in0=MCOL[:, 2:3], in1=MCOL[:, 0:1],
                                    op=OP.subtract)
                    vgap(e)
                    e.tensor_scalar(out=MCOL[:, 1:2], in0=MCOL[:, 1:2], scalar1=LN_EPS,
                                    scalar2=None, op0=OP.add)
                    vgap(e)
                    e.tensor_copy(out=JUNK[:, 0:128], in_=JUNK[:, 0:128]).then_inc(ve.h, 1)
                block.vector(b3_comb)
                ve.n += 1
                ve_c = ve.n

                def b3_rstd(e, ve_c=ve_c):
                    e.wait_ge(ve.h, ve_c)
                    e.activation(out=MCOL[:, 3:4], in_=MCOL[:, 1:2], func=AF.Ln)
                    sgap(e)
                    e.activation(out=MCOL[:, 5:6], in_=MCOL[:, 3:4], func=AF.Exp, scale=-0.5)
                    sgap(e)
                    e.copy(out=JUNK[:, 0:128], in_=JUNK[:, 0:128]).then_inc(sc.h, 1)
                block.scalar(b3_rstd)
                sc.n += 1
                sc_r = sc.n

                def b3_ap(e, sc_r=sc_r):
                    e.wait_ge(sc.h, sc_r)
                    e.tensor_scalar(out=PK[:, 1024:2048], in0=PK[:, 0:1024],
                                    scalar1=STAT[:, 24:25], scalar2=MCOL[:, 5:6],
                                    op0=OP.subtract, op1=OP.mult)
                    vgap(e)
                    e.tensor_copy(out=JUNK[:, 0:128], in_=JUNK[:, 0:128]).then_inc(ve.h, 1)
                block.vector(b3_ap)
                ve.n += 1
                G[("b3_ap", tt)] = ve.n
                ve_ap = ve.n

                def b3_tr2(e, tt=tt, ve_ap=ve_ap):
                    e.wait_ge(ve.h, ve_ap)
                    if G.get(("b3_ev2", tt - 1)) is not None:
                        e.wait_ge(ve.h, G[("b3_ev2", tt - 1)])
                    for half in range(2):
                        ps = [PS0, PS1][half]
                        for j in range(4):
                            ct = 4 * half + j
                            ins = e.transpose(ps[:, 128 * j:128 * (j + 1)],
                                              PK[:, 1024 + 128 * ct:1024 + 128 * (ct + 1)],
                                              IDENT[:])
                    ins.then_inc(mm.h, 1)
                block.tensor(b3_tr2)
                mm.n += 1
                mm_t2 = mm.n

                def b3_ev2(e, tt=tt, mm_t2=mm_t2):
                    e.wait_ge(mm.h, mm_t2)
                    for half in range(2):
                        ps = [PS0, PS1][half]
                        dst = VAUG[:].rearrange("p (q t) -> p q t", t=2080)[
                            :, 4 * half:4 * half + 4, 128 * tt:128 * (tt + 1)] if False \
                            else None
                    for half in range(2):
                        ps = [PS0, PS1][half]
                        dst = VAUG[:, 0:16384].rearrange("p (q t) -> p q t", t=2048)[
                            :, 4 * half:4 * half + 4, 128 * tt:128 * (tt + 1)]
                        ins = e.copy(out=dst, in_=ps[:].rearrange("p (b t) -> p b t", t=128))
                    ins.then_inc(sc.h, 1)
                block.scalar(b3_ev2)
                sc.n += 1
                sc_e2 = sc.n

                def b3_g2(e, tt=tt, sc_e2=sc_e2):
                    e.wait_ge(sc.h, sc_e2)
                    for ct in range(8):
                        ins = e.tensor_scalar(
                            out=VAUG[:, 2048 * ct + 128 * tt:2048 * ct + 128 * (tt + 1)],
                            in0=VAUG[:, 2048 * ct + 128 * tt:2048 * ct + 128 * (tt + 1)],
                            scalar1=PRM[:, 16 + ct:17 + ct], scalar2=PRM[:, 24 + ct:25 + ct],
                            op0=OP.mult, op1=OP.add)
                    ins.then_inc(ve.h, 1)
                block.vector(b3_g2)
                ve.n += 1
                G[("b3_ev2", tt)] = ve.n
            VE_XN = ve.n

            if phase == "dbgb":
                def dbgxn(e):
                    e.wait_ge(ve.h, VE_XN)
                    e.dma_start(out=dbg_xn[:], in_=VAUG[:, 0:16384]).then_inc(st.h, 16)
                block.sync(dbgxn)
                st.n += 16

            def emit_kk_wld(jt):
                def kk_wld(e, jt=jt):
                    wb_off = 1024 * (jt % 2)
                    if G.get(("kk_mm", jt - 2)) is not None:
                        e.wait_ge(mm.h, G[("kk_mm", jt - 2)])
                    w4 = d["W_cm"].rearrange("(hc p) (q j) -> p hc q j", p=128, j=128)
                    e.dma_start(out=WR0[:, wb_off:wb_off + 1024].rearrange(
                        "p (hc j) -> p hc j", j=128), in_=w4[:, :, jt, :]).then_inc(ld.h, 16)
                block.sync(kk_wld)
                ld.n += 16
                G[("kk_ldw", jt)] = ld.n

            emit_kk_wld(0)
            for jt in range(32):
                ct8 = jt % 8
                wb_off = 1024 * (jt % 2)
                ld_w = G[("kk_ldw", jt)]

                for tc in range(4):
                    ps = PSS[tc % 2]

                    def kk_mm(e, tc=tc, ps=ps, jt=jt, ld_w=ld_w, wb_off=wb_off):
                        e.wait_ge(ld.h, ld_w + (16 if jt == 0 else 0))
                        if jt == 0 and tc == 0:
                            e.wait_ge(ve.h, VE_XN)
                        gk = ("kk_ev", 4 * jt + tc - 2)
                        if G.get(gk) is not None:
                            e.wait_ge(sc.h, G[gk])
                        for hc in range(8):
                            ins = e.matmul(
                                ps[:], WR0[:, wb_off + 128 * hc:wb_off + 128 * (hc + 1)],
                                VAUG[:, 2048 * hc + 512 * tc:2048 * hc + 512 * (tc + 1)],
                                start=(hc == 0), stop=(hc == 7))
                        ins.then_inc(mm.h, 1)
                    block.tensor(kk_mm)
                    mm.n += 1
                    G[("kk_mm", jt)] = mm.n
                    mm_k = mm.n

                    def kk_ev(e, tc=tc, ps=ps, jt=jt, mm_k=mm_k):
                        e.wait_ge(mm.h, mm_k)
                        if tc == 0 and G.get(("kk_sp", jt - 1)) is not None:
                            e.wait_ge(st.h, G[("kk_sp", jt - 1)])
                        e.copy(out=PQ[:, 512 * tc:512 * (tc + 1)], in_=ps[:]).then_inc(sc.h, 1)
                    block.scalar(kk_ev)
                    sc.n += 1
                    G[("kk_ev", 4 * jt + tc)] = sc.n
                sc_t = sc.n
                if jt + 1 < 32:
                    emit_kk_wld(jt + 1)

                def kk_mix(e, jt=jt, ct8=ct8, sc_t=sc_t):
                    e.wait_ge(sc.h, sc_t)
                    e.tensor_copy(out=PA[:, 1:2048],
                                  in_=LNT[:, 2048 * ct8:2048 * ct8 + 2047])
                    e.tensor_copy(out=PA[:, 0:1],
                                  in_=LNT[:, 2048 * ct8 + 2047:2048 * ct8 + 2048])
                    vgap(e)
                    cmkcol = PRM[:, 72 + ct8:73 + ct8]
                    e.tensor_tensor(out=PK[:, 0:2048], in0=PQ[:], in1=PA[:, 0:2048],
                                    op=OP.subtract)
                    e.scalar_tensor_tensor(out=PQ[:], in0=PK[:, 0:2048], scalar=cmkcol,
                                           in1=PA[:, 0:2048], op0=OP.mult, op1=OP.add)
                    e.tensor_scalar(out=PK[:, 0:2048], in0=PQ[:], scalar1=0.0, scalar2=None,
                                    op0=OP.max)
                    vgap(e)
                    e.tensor_copy(out=JUNK[:, 0:128], in_=JUNK[:, 0:128]).then_inc(ve.h, 1)
                block.vector(kk_mix)
                ve.n += 1
                ve_m = ve.n

                def kk_sq(e, ve_m=ve_m):
                    e.wait_ge(ve.h, ve_m)
                    e.activation(out=PQ[:], in_=PK[:, 0:2048], func=AF.Square).then_inc(sc.h, 1)
                block.scalar(kk_sq)
                sc.n += 1
                sc_sq2 = sc.n

                def kk_sp(e, jt=jt, sc_sq2=sc_sq2):
                    e.wait_ge(sc.h, sc_sq2)
                    e.dma_start(out=kkms.rearrange("(c p) t -> c p t", p=128)[jt],
                                in_=PQ[:]).then_inc(st.h, 16)
                block.sync(kk_sp)
                st.n += 16
                G[("kk_sp", jt)] = st.n
            ST_KK = st.n

            if phase in ("dbgb", "fullx3"):
                def dbgb3(e):
                    e.wait_ge(st.h, ST_KK)
                    for cti in range(32):
                        e.dma_start(out=dbg_kk.rearrange("(c p) t -> c p t", p=128)[cti],
                                    in_=kkms.rearrange("(c p) t -> c p t", p=128)[cti]
                                    ).then_inc(st.h, 16)
                block.sync(dbgb3)
                st.n += 16 * 32

            strip_slots = ([("EXP", 512 * i) for i in range(16)]
                           + [("PQ", 512 * i) for i in range(4)]
                           + [("PK", 512 * i) for i in range(4)]
                           + [("PA", 512 * i) for i in range(4)]
                           + [("WR0", 512 * i) for i in range(4)])

            def strip_ap(i):
                nmb, off = strip_slots[i]
                buf = {"EXP": EXP, "PQ": PQ, "PK": PK, "PA": PA, "WR0": WR0}[nmb]
                return buf[:, off:off + 512]

            w_cp4 = d["W_cp"].rearrange("(ic p) (q j) -> p ic q j", p=128, j=128)
            for tc in range(4):
                def vv_sld(e, tc=tc):
                    e.wait_ge(st.h, ST_KK)
                    if tc == 0:
                        e.wait_ge(ve.h, ve.n)
                        e.wait_ge(sc.h, sc.n)
                    if G.get(("vv_evdone", tc - 1)) is not None:
                        e.wait_ge(ve.h, G[("vv_evdone", tc - 1)])
                    kk4 = kkms.rearrange("(c p) (q j) -> c p q j", p=128, j=512)
                    for i in range(32):
                        e.dma_start(out=strip_ap(i), in_=kk4[i][:, tc, :]).then_inc(ld.h, 16)
                block.sync(vv_sld)
                ld.n += 16 * 32
                ld_s = ld.n

                for hc in range(8):
                    psv = PSS[hc % 2]
                    for grp in range(8):
                        wo = 512 * (grp % 2)

                        def vv_wld(e, tc=tc, hc=hc, grp=grp, wo=wo):
                            gk = ("vv_mmg", tc, hc, grp - 2)
                            if G.get(gk) is not None:
                                e.wait_ge(mm.h, G[gk])
                            elif grp < 2:
                                e.wait_ge(mm.h, G.get("vv_last_mm", 0))
                            e.dma_start(
                                out=WR1[:, wo:wo + 512].rearrange("p (ic j) -> p ic j", j=128),
                                in_=w_cp4[:, 4 * grp:4 * grp + 4, hc, :]).then_inc(ld.h, 16)
                        block.sync(vv_wld)
                        ld.n += 16
                        ld_wg = ld.n

                        def vv_mm(e, tc=tc, hc=hc, grp=grp, wo=wo, psv=psv, ld_wg=ld_wg,
                                  ld_s=ld_s):
                            if grp == 0:
                                e.wait_ge(ld.h, ld_wg + 16)
                            else:
                                e.wait_ge(ld.h, ld_wg)
                            if grp == 0:
                                e.wait_ge(ld.h, ld_s)
                                gk = ("vv_ev", tc, hc - 2)
                                if G.get(gk) is not None:
                                    e.wait_ge(ve.h, G[gk])
                                elif hc < 2 and G.get(("vv_ev_prev", tc)) is not None:
                                    e.wait_ge(ve.h, G[("vv_ev_prev", tc)])
                            for q in range(4):
                                i = 4 * grp + q
                                ins = e.matmul(psv[:], WR1[:, wo + 128 * q:wo + 128 * (q + 1)],
                                               strip_ap(i), start=(i == 0), stop=(i == 31))
                            ins.then_inc(mm.h, 1)
                        block.tensor(vv_mm)
                        mm.n += 1
                        G[("vv_mmg", tc, hc, grp)] = mm.n
                    G["vv_last_mm"] = mm.n
                    mm_vv = mm.n

                    JT = JUNK[:, 128:640]
                    base = 2048 * hc + 512 * tc
                    if tc > 0:
                        xpAP = LNT[:, base - 1:base + 511]
                        lo_, wid = 0, 512
                    else:
                        xpAP = LNT[:, base:base + 511]
                        lo_, wid = 1, 511

                    def vv_jt(e, psv=psv, tc=tc, hc=hc, mm_vv=mm_vv):
                        e.wait_ge(mm.h, mm_vv)
                        if G.get("vv_ev_last") is not None:
                            e.wait_ge(ve.h, G["vv_ev_last"])
                        if phase == "dbgb" and tc == 0:
                            e.copy(out=VAUG[:, 16400 + hc:16401 + hc],
                                   in_=VAUG[:, 10240:10241])
                            sgap(e)
                        e.copy(out=JT, in_=psv[:]).then_inc(sc.h, 1)
                    block.scalar(vv_jt)
                    sc.n += 1
                    G[("vv_jt", tc, hc)] = sc.n
                    sc_jt = sc.n

                    def vv_rr1(e, tc=tc, hc=hc, base=base, xpAP=xpAP, lo_=lo_, wid=wid,
                              sc_jt=sc_jt):
                        e.wait_ge(sc.h, sc_jt)
                        e.tensor_tensor(out=PS2[:, lo_:lo_ + wid],
                                        in0=VAUG[:, base + lo_:base + lo_ + wid],
                                        in1=xpAP, op=OP.subtract)
                        e.scalar_tensor_tensor(out=PS3[:, lo_:lo_ + wid],
                                               in0=PS2[:, lo_:lo_ + wid],
                                               scalar=PRM[:, 80 + hc:81 + hc],
                                               in1=xpAP, op0=OP.mult, op1=OP.add)
                        if phase == "dbgb" and tc == 0 and hc == 5:
                            e.tensor_copy(out=VAUG[:, 16412:16413], in_=VAUG[:, 10240:10241])
                        vgap(e)
                        e.tensor_copy(out=JUNK[:, 0:128], in_=JUNK[:, 0:128]).then_inc(ve.h, 1)
                    block.vector(vv_rr1)
                    ve.n += 1
                    ve_r1 = ve.n

                    def vv_sig(e, lo_=lo_, wid=wid, ve_r1=ve_r1):
                        e.wait_ge(ve.h, ve_r1)
                        e.activation(out=PS2[:, lo_:lo_ + wid], in_=PS3[:, lo_:lo_ + wid],
                                     func=AF.Sigmoid).then_inc(sc.h, 1)
                    block.scalar(vv_sig)
                    sc.n += 1
                    sc_sg = sc.n

                    def vv_ev(e, tc=tc, hc=hc, base=base, lo_=lo_, wid=wid, sc_sg=sc_sg):
                        e.wait_ge(sc.h, sc_sg)
                        if phase == "dbgb" and tc == 0 and hc == 5:
                            e.tensor_copy(out=VAUG[:, 16413:16414], in_=VAUG[:, 10240:10241])
                        e.tensor_tensor(out=PS3[:, lo_:lo_ + wid], in0=PS2[:, lo_:lo_ + wid],
                                        in1=JT[:, lo_:lo_ + wid], op=OP.mult)
                        e.tensor_tensor(out=VAUG[:, base + lo_:base + lo_ + wid],
                                        in0=PS3[:, lo_:lo_ + wid],
                                        in1=LNT[:, base + lo_:base + lo_ + wid], op=OP.add)
                        if phase == "dbgb" and tc == 0 and hc == 5:
                            e.tensor_copy(out=VAUG[:, 16414:16415], in_=VAUG[:, 10240:10241])
                        vgap(e)
                        e.tensor_copy(out=JUNK[:, 0:128], in_=JUNK[:, 0:128]).then_inc(ve.h, 1)
                    block.vector(vv_ev)
                    ve.n += 1
                    G[("vv_ev", tc, hc)] = ve.n
                    G["vv_ev_last"] = ve.n

                    if tc == 0:
                        def vv_f1(e, hc=hc, base=base):
                            e.wait_ge(ve.h, ve.n)
                            wrap = 2048 * hc + 2047
                            e.tensor_tensor(out=PS2[:, 4:5], in0=VAUG[:, base:base + 1],
                                            in1=LNT[:, wrap:wrap + 1], op=OP.subtract)
                            vgap(e)
                            e.scalar_tensor_tensor(out=PS3[:, 4:5], in0=PS2[:, 4:5],
                                                   scalar=PRM[:, 80 + hc:81 + hc],
                                                   in1=LNT[:, wrap:wrap + 1],
                                                   op0=OP.mult, op1=OP.add)
                            vgap(e)
                            e.tensor_copy(out=JUNK[:, 0:128],
                                          in_=JUNK[:, 0:128]).then_inc(ve.h, 1)
                        block.vector(vv_f1)
                        ve.n += 1
                        ve_f1 = ve.n

                        def vv_f2(e, ve_f1=ve_f1):
                            e.wait_ge(ve.h, ve_f1)
                            e.activation(out=PS2[:, 5:6], in_=PS3[:, 4:5], func=AF.Sigmoid)
                            sgap(e)
                            e.copy(out=JUNK[:, 0:128], in_=JUNK[:, 0:128]).then_inc(sc.h, 1)
                        block.scalar(vv_f2)
                        sc.n += 1
                        sc_f2 = sc.n

                        def vv_f3(e, base=base, sc_f2=sc_f2, hc=hc):
                            e.wait_ge(sc.h, sc_f2)
                            if phase == "dbgb" and hc == 5:
                                wrap = 2048 * hc + 2047
                                e.tensor_copy(out=VAUG[:, 16384:16385], in_=PS2[:, 4:5])
                                e.tensor_copy(out=VAUG[:, 16385:16386], in_=PS3[:, 4:5])
                                e.tensor_copy(out=VAUG[:, 16386:16387], in_=PS2[:, 5:6])
                                e.tensor_copy(out=VAUG[:, 16387:16388], in_=JT[:, 0:1])
                                e.tensor_copy(out=VAUG[:, 16388:16389], in_=VAUG[:, base:base + 1])
                                e.tensor_copy(out=VAUG[:, 16389:16390], in_=LNT[:, wrap:wrap + 1])
                                vgap(e)
                            e.tensor_tensor(out=PS3[:, 5:6], in0=PS2[:, 5:6], in1=JT[:, 0:1],
                                            op=OP.mult)
                            vgap(e)
                            e.tensor_tensor(out=VAUG[:, base:base + 1], in0=PS3[:, 5:6],
                                            in1=LNT[:, base:base + 1], op=OP.add)
                            vgap(e)
                            e.tensor_copy(out=JUNK[:, 0:128],
                                          in_=JUNK[:, 0:128]).then_inc(ve.h, 1)
                        block.vector(vv_f3)
                        ve.n += 1
                        G["vv_ev_last"] = ve.n
                G[("vv_ev_prev", tc)] = ve.n
                G[("vv_evdone", tc)] = ve.n
            VE_VV = ve.n

            if phase in ("dbgb", "fullx3"):
                def dbgb4(e):
                    e.wait_ge(ve.h, VE_VV)
                    e.dma_start(out=dbg_xb2[:], in_=VAUG[:]).then_inc(st.h, 16)
                    if phase == "dbgb":
                        e.dma_start(out=dbg_fix[:], in_=VAUG[:, 16384:16392]).then_inc(st.h, 16)
                        e.dma_start(out=dbg_xn[:, 0:20], in_=VAUG[:, 16396:16416]).then_inc(st.h, 16)
                block.sync(dbgb4)
                st.n += 48 if phase == "dbgb" else 16

            for tt in range(16):
                po = 1024 * (tt % 2)

                def b4_ld(e, tt=tt, po=po):
                    if tt == 0:
                        e.wait_ge(ve.h, VE_VV)
                    if G.get(("b4_st", tt - 2)) is not None:
                        e.wait_ge(st.h, G[("b4_st", tt - 2)])
                        e.wait_ge(ve.h, G[("b4_ev", tt - 2)])
                    e.dma_start(out=PK[:, po:po + 1024], in_=xin_t[tt]).then_inc(ld.h, 16)
                block.sync(b4_ld)
                ld.n += 16
                ld_x2 = ld.n

                def b4_tr(e, tt=tt):
                    if tt == 0:
                        e.wait_ge(ve.h, VE_VV)
                    if G.get(("b4_evv", tt - 1)) is not None:
                        e.wait_ge(ve.h, G[("b4_evv", tt - 1)])
                    for half in range(2):
                        ps = [PS2, PS3][half]
                        for j in range(4):
                            ct = 4 * half + j
                            ins = e.transpose(
                                ps[:, 128 * j:128 * (j + 1)],
                                VAUG[:, 2048 * ct + 128 * tt:2048 * ct + 128 * (tt + 1)],
                                IDENT[:])
                    ins.then_inc(mm.h, 1)
                block.tensor(b4_tr)
                mm.n += 1
                mm_t = mm.n

                def b4_evs(e, tt=tt, mm_t=mm_t):
                    e.wait_ge(mm.h, mm_t)
                    for half in range(2):
                        ps = [PS2, PS3][half]
                        dst = PQ[:].rearrange("p (b u) -> p b u", u=128)[
                            :, 4 * half:4 * half + 4, :] if half == 0 else \
                            PQ[:].rearrange("p (b u) -> p b u", u=128)[
                            :, 4 * half:4 * half + 4, :]
                        ins = e.copy(out=dst, in_=ps[:].rearrange("p (b u) -> p b u", u=128))
                    ins.then_inc(sc.h, 1)
                block.scalar(b4_evs)
                sc.n += 1
                sc_ev = sc.n

                def b4_ev(e, tt=tt, po=po, sc_ev=sc_ev, ld_x2=ld_x2):
                    e.wait_ge(sc.h, sc_ev)
                    e.wait_ge(ld.h, ld_x2)
                    e.tensor_tensor(out=PK[:, po:po + 1024], in0=PQ[:, 0:1024],
                                    in1=PK[:, po:po + 1024], op=OP.add)
                    vgap(e)
                    e.tensor_copy(out=JUNK[:, 0:128], in_=JUNK[:, 0:128]).then_inc(ve.h, 1)
                block.vector(b4_ev)
                ve.n += 1
                G[("b4_ev", tt)] = ve.n
                G[("b4_evv", tt)] = ve.n
                ve_x3 = ve.n

                def b4_st(e, tt=tt, po=po, ve_x3=ve_x3, write_out=write_out, xout_t=xout_t):
                    e.wait_ge(ve.h, ve_x3)
                    e.dma_start(out=xout_t[tt], in_=PK[:, po:po + 1024]).then_inc(st.h, 16)
                    if write_out:
                        e.dma_start(out=out_t[tt], in_=PK[:, po:po + 1024]).then_inc(st.h, 16)
                block.sync(b4_st)
                st.n += 32 if write_out else 16
                G[("b4_st", tt)] = st.n

        if phase in ("x3", "full", "dbgb", "fullx3", "fullx2"):
            emit_attnb(x2s_t, x3s_t, write_out=(phase in ("x3", "dbgb", "fullx3")))

        if phase == "full":
            emit_ffn(2, x3s_t, out_t, write_out=False)
        elif phase in ("fullx3", "fullx2"):
            emit_ffn(2, x3s_t, x1s_t, write_out=False)

        def fin(e):
            e.wait_ge(st.h, st.n)
            e.wait_ge(gs.h, gs.n)
        block.sync(fin)

        counters = {"s_gs": gs.n,
                    "s_ld": ld.n, "s_mm": mm.n, "s_sc": sc.n, "s_ve": ve.n,
                    "s_st": st.n, "s_gp": gp.n}
    return d, out, counters


def _prm(inputs):
    names = ["g1", "b1", "g2", "b2", "time_decay", "time_first", "tmk", "tmv", "tmr",
             "cmk", "cmr"]
    cols = []
    for nm in names:
        v = np.asarray(inputs["L0_attn_b_" + nm], dtype=np.float32).ravel()
        cols.append(np.ascontiguousarray(v.reshape(8, 128).T))
    prm = np.concatenate(cols, axis=1)
    return np.ascontiguousarray(np.concatenate(
        [prm, np.zeros((128, 8), np.float32)], axis=1))


def kernel(**inputs):
    b = 8
    nc = bass.Bass("TRN2", target_bir_lowering=False, detect_race_conditions=False)
    build(nc, BUILD_PHASE)

    x = np.ascontiguousarray(inputs["L0_attn_a_x"], dtype=np.float32)
    com = {
        "wqT": np.ascontiguousarray(inputs["L0_attn_a_Wq"].T),
        "wkT": np.ascontiguousarray(inputs["L0_attn_a_Wk"].T),
        "wvT": np.ascontiguousarray(inputs["L0_attn_a_Wv"].T),
        "woT": np.ascontiguousarray(inputs["L0_attn_a_Wo"].T),
        "bq": np.ascontiguousarray(inputs["L0_attn_a_bq"].reshape(8, 128).T),
        "bk": np.ascontiguousarray(inputs["L0_attn_a_bk"].reshape(8, 128).T),
        "bv": np.ascontiguousarray(inputs["L0_attn_a_bv"].reshape(1, -1)),
        "bo": np.ascontiguousarray(inputs["L0_attn_a_bo"].reshape(1, -1)),
        "w_up": np.ascontiguousarray(inputs["L0_ffn_w_up"]),
        "w_gate": np.ascontiguousarray(inputs["L0_ffn_w_gate"]),
        "w_down": np.ascontiguousarray(inputs["L0_ffn_w_down"]),
        "W_tm": np.ascontiguousarray(inputs["L0_attn_b_W_tm"]),
        "W_cm": np.ascontiguousarray(inputs["L0_attn_b_W_cm"]),
        "W_cp": np.ascontiguousarray(inputs["L0_attn_b_W_cp"]),
        "prm": _prm(inputs),
        "iota": np.ascontiguousarray(np.tile(np.arange(T, dtype=np.float32), (128, 1))),
        "ident": np.eye(128, dtype=np.float32),
        "onesrow": np.ones((33, 128), dtype=np.float32),
    }
    in_maps = []
    for i in range(b):
        m = dict(com)
        m["x_tok"] = np.ascontiguousarray(x[i])
        m["xT"] = np.ascontiguousarray(x[i].T)
        in_maps.append(m)
    res = run_bass_kernel_spmd(nc, in_maps, core_ids=list(range(b)))
    return np.stack([r["out"] for r in res.results], axis=0)
